# version 03
# baseline (speedup 1.0000x reference)
import contextlib
import numpy as np
import concourse.bass as bass
import concourse.mybir as mybir
from concourse.bass_utils import run_bass_kernel_spmd

F32 = mybir.dt.float32
BF16 = mybir.dt.bfloat16
AF = mybir.ActivationFunctionType
ALU = mybir.AluOpType
AX = mybir.AxisListType

NCORES = 8
B = 4
T = 8192
D = 1024
TC = T // NCORES
NTOK = B * TC
NRT = NTOK // 128
NCH = TC // 128
H0, M0 = 4, 64
HK, HV = 128, 256
KS = 4
NE, HMOE = 14, 2048
HEADS, DH, WIN = 8, 64, 512
INNER = 512
HID1 = 2730
QK_SCALE = 8.0
MD8K = 8192


def _l0_rope_tables(T0, T1):
    inv_freq = 1.0 / 10000.0 ** (np.arange(0, HK, 2, dtype=np.float64) / HK)
    t = np.arange(T0, T1, dtype=np.float64)
    freqs = t[:, None] * inv_freq[None, :]
    emb = np.concatenate([freqs, freqs], -1)
    return np.cos(emb), np.sin(emb)


def _xpos_tables_host(P0, P1):
    half = np.arange(0, DH, 2, dtype=np.float64)
    inv_freq = 1.0 / 10000.0 ** (half / DH)
    t = np.arange(P0, P1, dtype=np.float64)
    freqs = t[:, None] * inv_freq[None, :]
    freqs = np.concatenate([freqs, freqs], -1)
    base = (half + 0.4 * DH) / (1.4 * DH)
    power = (t - T // 2) / float(WIN // 2)
    scale = base[None, :] ** power[:, None]
    scale = np.concatenate([scale, scale], -1)
    cq = (np.cos(freqs) * scale).T
    sq = (np.sin(freqs) * scale).T
    ck = (np.cos(freqs) / scale).T
    sk = (np.sin(freqs) / scale).T

    out = {}
    out["xcq"] = np.ascontiguousarray(np.tile(cq, (2, 1)).astype(np.float32))
    out["xsq"] = np.ascontiguousarray(np.tile(sq, (2, 1)).astype(np.float32))
    out["xck"] = np.ascontiguousarray(np.tile(ck, (2, 1)).astype(np.float32))
    out["xsk"] = np.ascontiguousarray(np.tile(sk, (2, 1)).astype(np.float32))
    return out


def _band_masks_host(first=False):
    import ml_dtypes
    out = np.zeros((4, 128, 1024), np.float32)
    for ic in range(4):
        i = ic * 128 + np.arange(128)[:, None]
        jl = np.arange(1024)[None, :]
        m = (jl >= i) & (jl <= i + 512)
        if first:
            m = m & (jl >= 512)
        out[ic] = m
    return np.ascontiguousarray(out.astype(ml_dtypes.bfloat16))


def host_core_inputs(core, inputs):
    import ml_dtypes
    d = {}
    x = inputs["L0_attn_hidden_states"]
    T0 = core * TC
    xs = np.asarray(x[:, T0:T0 + TC, :])
    d["x_slice"] = np.ascontiguousarray(xs.reshape(NTOK, D))
    d["xT_slice"] = np.ascontiguousarray(xs.reshape(NTOK, D).T)
    if core == 0:
        halo = np.zeros((B, 3, D), np.float32)
    else:
        halo = np.asarray(x[:, T0 - 3:T0, :])
    d["xT_halo"] = np.ascontiguousarray(halo.reshape(B * 3, D).T)

    for nm in ("Wq", "Wk", "Wv", "Ws", "Wg", "Wo"):
        d[nm] = np.ascontiguousarray(inputs["L0_attn_" + nm].astype(np.float32))
    for nm in ("qconv_w", "kconv_w", "vconv_w"):
        w = np.asarray(inputs["L0_attn_" + nm])[:, 0, :]
        ch = w.shape[0]
        d[nm[0] + "cw"] = np.ascontiguousarray(
            w.reshape(ch // 128, 128, KS).transpose(1, 0, 2).astype(np.float32))
    d["gnorm_rep"] = np.ascontiguousarray(np.broadcast_to(
        np.tile(np.asarray(inputs["L0_attn_gnorm_w"], dtype=np.float32), H0)[None, :],
        (128, H0 * HV)).copy())

    cos, sin = _l0_rope_tables(T0, T0 + TC)
    sgn = sin.T.copy()
    sgn[HK // 2:, :] *= -1.0
    d["ropec_q"] = np.ascontiguousarray((cos.T * HK ** -0.5).astype(np.float32))
    d["ropes_q"] = np.ascontiguousarray((sgn * HK ** -0.5).astype(np.float32))
    d["ropec_k"] = np.ascontiguousarray(cos.T.astype(np.float32))
    d["ropes_k"] = np.ascontiguousarray(sgn.astype(np.float32))

    d["identb"] = np.eye(128, dtype=ml_dtypes.bfloat16)
    d["identf"] = np.eye(128, dtype=np.float32)
    ut = (np.arange(128)[:, None] <= np.arange(128)[None, :])
    d["utmask"] = np.ascontiguousarray(ut.astype(np.float32))
    d["utmaskb"] = np.ascontiguousarray(ut.astype(ml_dtypes.bfloat16))
    d["onescol"] = np.ones((128, 1), dtype=ml_dtypes.bfloat16)
    d["ones512"] = np.ones((1, 512), dtype=ml_dtypes.bfloat16)
    d["onesrow"] = np.ones((1, 128), dtype=ml_dtypes.bfloat16)

    d["router_w"] = np.ascontiguousarray(inputs["L0_ffn_router_w"].astype(np.float32))
    d["w1"] = np.ascontiguousarray(inputs["L0_ffn_w1"])
    d["w2"] = np.ascontiguousarray(inputs["L0_ffn_w2"])
    b1 = np.asarray(inputs["L0_ffn_b1"])
    d["b1r"] = np.ascontiguousarray(b1.reshape(NE, 16, 128).transpose(0, 2, 1).astype(np.float32))
    d["b2"] = np.ascontiguousarray(inputs["L0_ffn_b2"].astype(np.float32))

    d["wqkv"] = np.ascontiguousarray(inputs["L1_ffn_wqkv"])
    d["wo1"] = np.ascontiguousarray(inputs["L1_ffn_wo"])
    d["qscale"] = np.ascontiguousarray(inputs["L1_ffn_q_scale"].astype(np.float32).reshape(2, 64, 1))
    d["kscale"] = np.ascontiguousarray(inputs["L1_ffn_k_scale"].astype(np.float32).reshape(2, 64, 1))
    d["lng"] = np.ascontiguousarray(inputs["L1_ffn_ln_g"].astype(np.float32))
    d["lnb"] = np.ascontiguousarray(inputs["L1_ffn_ln_b"].astype(np.float32))
    d["w1f"] = np.ascontiguousarray(inputs["L1_ffn_w1"])
    d["b1f"] = np.ascontiguousarray(inputs["L1_ffn_b1"].astype(np.float32))
    d["w2f"] = np.ascontiguousarray(inputs["L1_ffn_w2"])
    d["b2f"] = np.ascontiguousarray(inputs["L1_ffn_b2"].astype(np.float32))

    tb = _xpos_tables_host(T0 - 512, T0 + TC)
    tb["xcq"] = np.ascontiguousarray(tb["xcq"][:, 512:])
    tb["xsq"] = np.ascontiguousarray(tb["xsq"][:, 512:])
    d.update(tb)
    seg2 = np.zeros((128, 2), np.float32)
    seg2[0:64, 0] = 1.0
    seg2[64:128, 1] = 1.0
    d["seg2"] = seg2.astype(ml_dtypes.bfloat16)
    rep2 = np.zeros((2, 128), np.float32)
    rep2[0, 0:64] = 1.0
    rep2[1, 64:128] = 1.0
    d["rep2"] = rep2.astype(ml_dtypes.bfloat16)
    d["onesr64"] = np.ones((1, 64), dtype=ml_dtypes.bfloat16)
    for li in range(2):
        d[f"qscale2_{li}"] = np.ascontiguousarray(
            np.tile(np.asarray(inputs["L1_ffn_q_scale"][li], np.float32), 2).reshape(128, 1))
        d[f"kscale2_{li}"] = np.ascontiguousarray(
            np.tile(np.asarray(inputs["L1_ffn_k_scale"][li], np.float32), 2).reshape(128, 1))
        d[f"wqkv_{li}"] = np.ascontiguousarray(inputs["L1_ffn_wqkv"][li].astype(np.float32))
        d[f"wo_{li}"] = np.ascontiguousarray(inputs["L1_ffn_wo"][li].astype(np.float32))
        d[f"w1f_{li}"] = np.ascontiguousarray(inputs["L1_ffn_w1"][li].astype(np.float32))
        d[f"w2f_{li}"] = np.ascontiguousarray(inputs["L1_ffn_w2"][li].astype(np.float32))
        d[f"lgb_{li}"] = np.ascontiguousarray(np.stack([
            np.asarray(inputs["L1_ffn_ln_g"][li], np.float32),
            np.asarray(inputs["L1_ffn_ln_b"][li], np.float32)]))
        b1li = np.asarray(inputs["L1_ffn_b1"][li], np.float32)
        b1av = np.zeros((128, 22), np.float32)
        b1gv = np.zeros((128, 22), np.float32)
        for fc in range(22):
            rows = 42 if fc == 21 else 128
            b1av[0:rows, fc] = b1li[fc * 128:fc * 128 + rows]
            b1gv[0:rows, fc] = b1li[2730 + fc * 128:2730 + fc * 128 + rows]
        d[f"b1a_{li}"] = b1av
        d[f"b1g_{li}"] = b1gv
        d[f"b2r_{li}"] = np.ascontiguousarray(
            np.asarray(inputs["L1_ffn_b2"][li], np.float32).reshape(8, 128).T)
    d["bandm"] = _band_masks_host()
    d["bandm0"] = _band_masks_host(first=(core == 0))
    r64 = np.zeros((64, 64), np.float32)
    r64[:32, 32:] = -np.eye(32)
    r64[32:, :32] = np.eye(32)
    rr = np.zeros((128, 128), np.float32)
    rr[:64, :64] = r64
    rr[64:, 64:] = r64
    d["rotT"] = np.ascontiguousarray(rr.T.astype(ml_dtypes.bfloat16))
    return d


class Prog:
    def __init__(self, name):
        self.nc = bass.Bass("TRN2", target_bir_lowering=False)
        self.es = contextlib.ExitStack()
        self.name = name
        self.sems = {}
        self.cnt = {}
        self.inputs = []
        self.outputs = []

    def sem(self, name):
        if name not in self.sems:
            self.sems[name] = self.es.enter_context(self.nc.semaphore(name))
            self.cnt[name] = 0
        return self.sems[name]

    def inc(self, instr, name, k=1):
        instr.then_inc(self.sem(name), k)
        self.cnt[name] += k
        return self.cnt[name]

    def edge(self, instr, eng, tag="ve_self"):
        n = self.inc(instr, tag, 1)
        eng.wait_ge(self.sem(tag), n)

    def w(self, eng, name, lag=0):
        if name in self.cnt and self.cnt[name] - lag > 0:
            eng.wait_ge(self.sem(name), self.cnt[name] - lag)

    def din(self, name, shape, dt=F32):
        self.inputs.append(name)
        return self.nc.dram_tensor(name, list(shape), dt, kind="ExternalInput").ap()

    def dout(self, name, shape, dt=F32):
        self.outputs.append(name)
        return self.nc.dram_tensor(name, list(shape), dt, kind="ExternalOutput").ap()

    def sb(self, name, shape, dt=F32):
        return self.es.enter_context(self.nc.sbuf_tensor(name, list(shape), dt))

    def ps(self, name, shape, dt=F32):
        return self.es.enter_context(self.nc.psum_tensor(name, list(shape), dt))


def rearr_cp(ap):
    return ap.rearrange("(c p) n -> p c n", p=128)


def emit_phase1(P, io):
    nc = P.nc
    sy, gp, te, ve, sc = nc.sync, nc.gpsimd, nc.tensor, nc.vector, nc.scalar

    x_fm = P.sb("x_fm", [128, 8, NTOK], BF16)
    x_fm_halo = P.sb("x_fm_halo", [128, 8, 12], BF16)
    wq = P.sb("wqS", [128, 8, 512], BF16)
    wk = P.sb("wkS", [128, 8, 512], BF16)
    wv = P.sb("wvS", [128, 8, 1024], BF16)
    ws = P.sb("wsS", [128, 8, 256], BF16)
    cwq = P.sb("cwqS", [128, 4, KS], F32)
    cwk = P.sb("cwkS", [128, 4, KS], F32)
    cwv = P.sb("cwvS", [128, 8, KS], F32)
    rc_q = P.sb("rc_q", [128, TC], F32)
    rs_q = P.sb("rs_q", [128, TC], F32)
    rc_k = P.sb("rc_k", [128, TC], F32)
    rs_k = P.sb("rs_k", [128, TC], F32)
    identb = P.sb("identbS", [128, 128], BF16)
    onescol = P.sb("onescolS", [128, 1], BF16)

    pre = P.sb("pre", [128, B, TC + 3], F32)
    cvo = P.sb("cvo", [128, B, TC], F32)
    rp1 = P.sb("rp1", [128, TC], F32)
    obf = [P.sb(f"obf{i}", [128, NTOK], BF16) for i in range(2)]
    tstg = P.sb("tstg", [128, 2, 4, 128], BF16)
    est = P.sb("est", [128, 2, 256], BF16)
    ssb = P.sb("ssb", [128, 256], F32)
    ktt = P.sb("ktt", [128, 2, 512], BF16)
    vtt = P.sb("vtt", [128, 2, 1024], BF16)
    skI = P.sb("skI", [128, 256], F32)
    svI = P.sb("svI", [128, 2, 256], F32)
    ssI = P.sb("ssI", [1, 256], F32)
    skT = P.sb("skT", [128, 16, M0], F32)
    svT = P.sb("svT", [128, 8, HV], F32)
    ssT = P.sb("ssT", [1, B, 256], F32)

    PSf = P.ps("PSf", [128, 6, 512], F32)
    PSb = P.ps("PSb", [128, 2, 1024], BF16)

    for half in range(2):
        P.inc(gp.dma_start(out=x_fm[:, :, half * 2048:(half + 1) * 2048],
                           in_=rearr_cp(io["xT_slice"])[:, :, half * 2048:(half + 1) * 2048]),
              "ld", 16)
    P.inc(gp.dma_start(out=x_fm_halo[:, :, :], in_=rearr_cp(io["xT_halo"])), "ld", 16)
    P.inc(gp.dma_start(out=wq[:, :, :], in_=rearr_cp(io["Wq"])), "ld", 16)
    P.inc(gp.dma_start(out=wk[:, :, :], in_=rearr_cp(io["Wk"])), "ld", 16)
    P.inc(gp.dma_start(out=wv[:, :, :], in_=rearr_cp(io["Wv"])), "ld", 16)
    P.inc(gp.dma_start(out=ws[:, :, :], in_=rearr_cp(io["Ws"])), "ld", 16)
    for nm, buf in (("qcw", cwq), ("kcw", cwk), ("vcw", cwv)):
        P.inc(sy.dma_start(out=buf[:, :, :], in_=io[nm][:, :, :]), "ldh", 16)
    for nm, buf in (("ropec_q", rc_q), ("ropes_q", rs_q), ("ropec_k", rc_k),
                    ("ropes_k", rs_k), ("identb", identb), ("onescol", onescol)):
        P.inc(sy.dma_start(out=buf[:, :], in_=io[nm][:, :]), "ldh", 16)
    ve.wait_ge(P.sem("ld"), P.cnt["ld"])
    ve.wait_ge(P.sem("ldh"), P.cnt["ldh"])
    ve.memset(skT[:, :, :], 0)
    ve.memset(svT[:, :, :], 0)
    ve.memset(ssT[:, :, :], 0)
    mz = ve.memset(pre[:, :, :], 0)
    P.inc(mz, "ini", 1)
    te.wait_ge(P.sem("ini"), 1)
    sc.wait_ge(P.sem("ini"), 1)
    sc.wait_ge(P.sem("ld"), P.cnt["ld"])
    sc.wait_ge(P.sem("ldh"), P.cnt["ldh"])

    specs = [("q", wq, cwq, 4, rc_q, rs_q, io["qf"], None),
             ("k", wk, cwk, 4, rc_k, rs_k, io["kf"], io["ktm"]),
             ("v", wv, cwv, 8, None, None, None, io["vtm"])]
    n_tr_groups = 0
    fc_seen = 0
    for nm, wsb, cw, nf, rcT, rsT, out_fm, out_tm in specs:
        for fc in range(nf):
            ob = obf[fc_seen % 2]
            for b in range(B):
                for tg in range(2):
                    bank = (b * 2 + tg) % 2
                    if P.cnt.get("evq", 0) >= 2:
                        te.wait_ge(P.sem("evq"), P.cnt["evq"] - 1)
                    for dc in range(8):
                        mi = te.matmul(PSf[:, bank, :], wsb[:, dc, fc * 128:(fc + 1) * 128],
                                       x_fm[:, dc, b * TC + tg * 512: b * TC + (tg + 1) * 512],
                                       start=(dc == 0), stop=(dc == 7))
                    P.inc(mi, "mmq", 1)
                    sc.wait_ge(P.sem("mmq"), P.cnt["mmq"])
                    if P.cnt.get("cnv", 0):
                        sc.wait_ge(P.sem("cnv"), P.cnt["cnv"])
                    ev = sc.copy(out=pre[:, b, 3 + tg * 512:3 + (tg + 1) * 512], in_=PSf[:, bank, :])
                    P.inc(ev, "evq", 1)
                for dc in range(8):
                    mi = te.matmul(PSf[:, 2, 0:3], wsb[:, dc, fc * 128:(fc + 1) * 128],
                                   x_fm_halo[:, dc, b * 3:(b + 1) * 3],
                                   start=(dc == 0), stop=(dc == 7))
                P.inc(mi, "mmq", 1)
                sc.wait_ge(P.sem("mmq"), P.cnt["mmq"])
                ev = sc.copy(out=pre[:, b, 0:3], in_=PSf[:, 2, 0:3])
                P.inc(ev, "evq", 1)
            ve.wait_ge(P.sem("evq"), P.cnt["evq"])
            if P.cnt.get("slu", 0):
                ve.wait_ge(P.sem("slu"), P.cnt["slu"])
            c0 = ve.tensor_scalar(out=cvo[:, :, :], in0=pre[:, :, 0:TC], scalar1=cw[:, fc, 0:1],
                                  scalar2=None, op0=ALU.mult, op1=ALU.bypass)
            P.edge(c0, ve)
            for j in range(1, KS):
                cv = ve.scalar_tensor_tensor(out=cvo[:, :, :], in0=pre[:, :, j:j + TC],
                                             scalar=cw[:, fc, j:j + 1], in1=cvo[:, :, :],
                                             op0=ALU.mult, op1=ALU.add)
                if j < KS - 1:
                    P.edge(cv, ve)
            P.inc(cv, "cnv", 1)
            sc.wait_ge(P.sem("cnv"), P.cnt["cnv"])
            si = sc.activation(out=cvo[:, :, :], in_=cvo[:, :, :], func=AF.Silu)
            P.inc(si, "slu", 1)
            ve.wait_ge(P.sem("slu"), P.cnt["slu"])
            qs_sem = f"qkst{fc_seen % 2}"
            if P.cnt.get(qs_sem, 0) > 0:
                ve.wait_ge(P.sem(qs_sem), P.cnt[qs_sem])
            if P.cnt.get("tev", 0):
                ve.wait_ge(P.sem("tev"), P.cnt["tev"])
            if rcT is not None:
                for b in range(B):
                    ve.tensor_tensor(out=rp1[0:64, :], in0=cvo[64:128, b, :],
                                     in1=rsT[64:128, :], op=ALU.mult)
                    r2 = ve.tensor_tensor(out=rp1[64:128, :], in0=cvo[0:64, b, :],
                                          in1=rsT[0:64, :], op=ALU.mult)
                    P.edge(r2, ve)
                    tmul = ve.tensor_tensor(out=cvo[:, b, :], in0=cvo[:, b, :], in1=rcT[:, :], op=ALU.mult)
                    P.edge(tmul, ve)
                    cb = ve.tensor_tensor(out=ob[:, b * TC:(b + 1) * TC],
                                          in0=cvo[:, b, :], in1=rp1[:, :], op=ALU.add)
                    P.inc(cb, "cst", 1)
                    ve.wait_ge(P.sem("cst"), P.cnt["cst"])
            else:
                cb = ve.tensor_copy(out=ob[:, :], in_=cvo[:, :, :].rearrange("p b n -> p (b n)"))
                P.inc(cb, "cst", 1)
            if out_fm is not None:
                sy.wait_ge(P.sem("cst"), P.cnt["cst"])
                P.inc(sy.dma_start(out=out_fm[fc, :, :], in_=ob[:, :]), qs_sem, 16)
            if out_tm is not None:
                te.wait_ge(P.sem("cst"), P.cnt["cst"])
                for rt in range(NRT):
                    g = rt // 4
                    slot = rt % 4
                    if P.cnt.get("tev", 0) >= 2:
                        te.wait_ge(P.sem("tev"), P.cnt["tev"] - 1)
                    tr = te.transpose(PSb[:, rt % 2, 0:128], ob[:, rt * 128:(rt + 1) * 128],
                                      identb[:, :])
                    P.inc(tr, "ttr", 1)
                    sc.wait_ge(P.sem("ttr"), P.cnt["ttr"])
                    ts_sem = f"tst{n_tr_groups % 2}"
                    if slot == 0 and P.cnt.get(ts_sem, 0) > 0:
                        sc.wait_ge(P.sem(ts_sem), P.cnt[ts_sem])
                    evk = sc.copy(out=tstg[:, n_tr_groups % 2, slot, :], in_=PSb[:, rt % 2, 0:128])
                    P.inc(evk, "tev", 1)
                    if slot == 3:
                        gp.wait_ge(P.sem("tev"), P.cnt["tev"])
                        dst = out_tm.rearrange("(r p) c -> p r c", p=128)[
                            :, g * 4:(g + 1) * 4, fc * 128:(fc + 1) * 128]
                        st2 = gp.dma_start(out=dst, in_=tstg[:, n_tr_groups % 2, :, :])
                        P.inc(st2, ts_sem, 16)
                        n_tr_groups += 1
            fc_seen += 1

    te.wait_ge(P.sem("evq"), P.cnt["evq"])
    te.wait_ge(P.sem("tev"), P.cnt["tev"])

    for s in ("tst0", "tst1"):
        if P.cnt.get(s, 0):
            gp.wait_ge(P.sem(s), P.cnt[s])
    ve.wait_ge(P.sem("ini"), 1)
    for rt in range(NRT):
        b = rt // NCH
        bank = rt % 2
        if rt >= 2:
            te.wait_ge(P.sem("sev"), rt - 1)
        for dc in range(8):
            mi = te.matmul(PSf[:, bank, 0:256], x_fm[:, dc, rt * 128:(rt + 1) * 128],
                           ws[:, dc, :], start=(dc == 0), stop=(dc == 7))
        P.inc(mi, "mms", 1)
        ve.wait_ge(P.sem("mms"), P.cnt["mms"])
        cl = ve.tensor_scalar(out=ssb[:, :], in0=PSf[:, bank, 0:256], scalar1=32.0,
                              scalar2=-32.0, op0=ALU.min, op1=ALU.max)
        P.inc(cl, "sev", 1)
        sc.wait_ge(P.sem("sev"), P.cnt["sev"])
        es_sem = f"est_st{rt % 2}"
        if rt >= 2:
            sc.wait_ge(P.sem("esu"), rt - 1)
            sc.wait_ge(P.sem(es_sem), P.cnt[es_sem])
        ex = sc.activation(out=est[:, rt % 2, :], in_=ssb[:, :], func=AF.Exp)
        P.inc(ex, "esx", 1)
        sy.wait_ge(P.sem("esx"), P.cnt["esx"])
        P.inc(sy.dma_start(out=io["es"][rt, :, :], in_=est[:, rt % 2, :]), es_sem, 16)
        if rt >= 2:
            gp.wait_ge(P.sem("esu"), rt - 1)
        kv_sem = f"kv{rt % 2}"
        P.inc(gp.dma_start(out=ktt[:, rt % 2, :],
                           in_=io["ktm"].rearrange("(r p) c -> p r c", p=128)[:, rt, :]), kv_sem, 16)
        P.inc(gp.dma_start(out=vtt[:, rt % 2, :],
                           in_=io["vtm"].rearrange("(r p) c -> p r c", p=128)[:, rt, :]), kv_sem, 16)
        te.wait_ge(P.sem(kv_sem), P.cnt[kv_sem])
        te.wait_ge(P.sem("esx"), P.cnt["esx"])
        if rt >= 1:
            te.wait_ge(P.sem("stv"), rt)
        for h in range(H0):
            mi = te.matmul(PSf[:, 3, h * 64:(h + 1) * 64],
                           ktt[:, rt % 2, h * 128:(h + 1) * 128],
                           est[:, rt % 2, h * 64:(h + 1) * 64], start=True, stop=True)
        P.inc(mi, "mst", 1)
        for h in range(H0):
            mi = te.matmul(PSf[0:64, 4 + h // 2, (h % 2) * 256:(h % 2) * 256 + 256],
                           est[:, rt % 2, h * 64:(h + 1) * 64],
                           vtt[:, rt % 2, h * 256:(h + 1) * 256], start=True, stop=True)
        P.inc(mi, "mst", 1)
        mi = te.matmul(PSf[0:1, 2, 0:256], onescol[:, :], est[:, rt % 2, :], start=True, stop=True)
        P.inc(mi, "mst", 1)
        ve.wait_ge(P.sem("mst"), P.cnt["mst"])
        if rt >= 1:
            ve.wait_ge(P.sem("ist"), (rt - 1) * 48 + 48)
        ve.tensor_copy(out=skI[:, :], in_=PSf[:, 3, 0:256])
        for h in range(H0):
            ve.tensor_copy(out=svI[(h % 2) * 64:(h % 2) * 64 + 64, h // 2, :],
                           in_=PSf[0:64, 4 + h // 2, (h % 2) * 256:(h % 2) * 256 + 256])
        cpi = ve.tensor_copy(out=ssI[:, :], in_=PSf[0:1, 2, 0:256])
        P.inc(cpi, "stv", 1)
        ve.wait_ge(P.sem("stv"), P.cnt["stv"])
        ve.tensor_tensor(out=skT[:, b * 4:(b + 1) * 4, :].rearrange("p h m -> p (h m)"),
                         in0=skT[:, b * 4:(b + 1) * 4, :].rearrange("p h m -> p (h m)"),
                         in1=skI[:, :], op=ALU.add)
        ve.tensor_tensor(out=svT[:, b * 2:(b + 1) * 2, :].rearrange("p h m -> p (h m)"),
                         in0=svT[:, b * 2:(b + 1) * 2, :].rearrange("p h m -> p (h m)"),
                         in1=svI[:, :, :].rearrange("p h m -> p (h m)"), op=ALU.add)
        tt = ve.tensor_tensor(out=ssT[:, b, :], in0=ssT[:, b, :], in1=ssI[:, :], op=ALU.add)
        P.inc(tt, "esu", 1)
        ve.wait_ge(P.sem("esu"), P.cnt["esu"])
        sy.wait_ge(P.sem("stv"), P.cnt["stv"])
        P.inc(sy.dma_start(out=io["skinc"][rt, :, :], in_=skI[:, :]), "ist", 16)
        P.inc(sy.dma_start(out=io["svinc"][rt, :, :],
                           in_=svI[:, :, :].rearrange("p h m -> p (h m)")), "ist", 16)
        P.inc(sy.dma_start(out=io["ssinc"][rt, :, :], in_=ssI[:, :]), "ist", 16)
    sy.wait_ge(P.sem("esu"), P.cnt["esu"])
    P.inc(sy.dma_start(out=io["sktot"][:, :], in_=skT[:, :, :].rearrange("p h m -> p (h m)")), "fst", 16)
    P.inc(sy.dma_start(out=io["svtot"][:, :], in_=svT[:, :, :].rearrange("p h m -> p (h m)")), "fst", 16)
    P.inc(sy.dma_start(out=io["sstot"][:, :], in_=ssT[:, :, :].rearrange("p b m -> p (b m)")), "fst", 16)
    for s in ("qkst0", "qkst1", "tst0", "tst1", "est_st0", "est_st1", "ist", "fst"):
        if P.cnt.get(s, 0):
            sy.wait_ge(P.sem(s), P.cnt[s])


def build_launch1():
    P = Prog("l1")
    io = {}
    io["xT_slice"] = P.din("xT_slice", [D, NTOK])
    io["xT_halo"] = P.din("xT_halo", [D, 12])
    for nm, shp in (("Wq", [D, 512]), ("Wk", [D, 512]), ("Wv", [D, D]), ("Ws", [D, 256])):
        io[nm] = P.din(nm, shp)
    io["qcw"] = P.din("qcw", [128, 4, KS])
    io["kcw"] = P.din("kcw", [128, 4, KS])
    io["vcw"] = P.din("vcw", [128, 8, KS])
    for nm in ("ropec_q", "ropes_q", "ropec_k", "ropes_k"):
        io[nm] = P.din(nm, [128, TC])
    io["identb"] = P.din("identb", [128, 128], BF16)
    io["onescol"] = P.din("onescol", [128, 1], BF16)
    io["qf"] = P.dout("qf", [4, 128, NTOK], BF16)
    io["kf"] = P.dout("kf", [4, 128, NTOK], BF16)
    io["ktm"] = P.dout("ktm", [NTOK, 512], BF16)
    io["vtm"] = P.dout("vtm", [NTOK, 1024], BF16)
    io["es"] = P.dout("es", [NRT, 128, 256], BF16)
    io["skinc"] = P.dout("skinc", [NRT, 128, 256])
    io["svinc"] = P.dout("svinc", [NRT, 128, 512])
    io["ssinc"] = P.dout("ssinc", [NRT, 1, 256])
    io["sktot"] = P.dout("sktot", [128, 16 * M0])
    io["svtot"] = P.dout("svtot", [128, 8 * HV])
    io["sstot"] = P.dout("sstot", [1, B * 256])
    with P.es:
        with P.nc.Block() as block:
            @block.sync
            def _(sy):
                emit_phase1(P, io)
    return P, io


def emit_phase2(P, io):
    nc = P.nc
    sy, gp, te, ve, sc = nc.sync, nc.gpsimd, nc.tensor, nc.vector, nc.scalar

    wg = P.sb("wgS", [128, 8, 1024], BF16)
    wo = P.sb("woS", [128, 8, 1024], BF16)
    gnr = P.sb("gnrS", [128, 1024], F32)
    utf = P.sb("utfS", [128, 128], F32)
    utb = P.sb("utbS", [128, 128], BF16)
    identb = P.sb("identbS", [128, 128], BF16)
    identf = P.sb("identfS", [128, 128], F32)
    onesrow = P.sb("onesrowS", [1, 128], BF16)
    rw = P.sb("rwS", [128, 8, 14], F32)
    rwh = P.sb("rwh", [128, 8, 14], BF16)
    rwl = P.sb("rwl", [128, 8, 14], BF16)
    SkR = P.sb("SkR", [128, 16, M0], F32)
    SkRb = P.sb("SkRb", [128, 16, M0], BF16)
    SvR = P.sb("SvR", [128, 8, HV], F32)
    SvRb = P.sb("SvRb", [64, 16, HV], BF16)
    SsR = P.sb("SsR", [1, B, 256], F32)
    SsRh = P.sb("SsRh", [1, B, 256], BF16)
    SsRl = P.sb("SsRl", [1, B, 256], BF16)
    qfc = P.sb("qfc", [128, 2, 4, 128], BF16)
    kfc = P.sb("kfc", [128, 2, 4, 128], BF16)
    esc = P.sb("esc", [128, 2, 256], BF16)
    vcc = P.sb("vcc", [128, 2, 1024], BF16)
    xcc = P.sb("xcc", [128, 2, 8, 128], BF16)
    xrow = P.sb("xrow", [128, 2, 1024], F32)
    skic = P.sb("skic", [128, 2, 256], F32)
    svic = P.sb("svic", [128, 2, 512], F32)
    ssic = P.sb("ssic", [1, 2, 256], F32)
    St = P.sb("St", [128, 256], F32)
    rS = P.sb("rS", [128, 256], F32)
    ptm = P.sb("ptm", [128, 4, 128], BF16)
    oks = P.sb("oks", [128, 256], F32)
    mx4 = P.sb("mx4", [128, 4], F32)
    nmx4 = P.sb("nmx4", [128, 4], F32)
    qv = P.sb("qv", [128, 256], F32)
    sm4 = P.sb("sm4", [128, 4], F32)
    rz4 = P.sb("rz4", [128, 4], F32)
    qts = P.sb("qts", [128, 256], F32)
    qtb = P.sb("qtb", [128, 256], BF16)
    esf = P.sb("esf", [64, 4, 128], BF16)
    qtf = P.sb("qtf", [64, 4, 128], BF16)
    atm = P.sb("atm", [128, 4, 128], BF16)
    o_sb = P.sb("o_sb", [128, 1024], F32)
    scr = P.sb("scr", [128, 256], BF16)
    ssq = P.sb("ssq", [128, 4], F32)
    rstd4 = P.sb("rstd4", [128, 4], F32)
    g_sb = P.sb("g_sb", [128, 1024], F32)
    ob2 = P.sb("ob2", [128, 1024], BF16)
    ofm = P.sb("ofm", [128, 8, 128], BF16)
    x1pre = P.sb("x1pre", [128, 1024], F32)
    bn6 = P.sb("bn6", [128, 6], F32)
    mv2 = P.sb("mv2", [128, 2], F32)
    rstdln = P.sb("rstdln", [128, 1], F32)
    x1row = P.sb("x1row", [128, 2, 1024], F32)
    x1th = P.sb("x1th", [128, 2, 8, 128], BF16)
    x1tl = P.sb("x1tl", [128, 2, 8, 128], BF16)
    lg_sb = P.sb("lg_sb", [128, 14], F32)
    emsk = P.sb("emsk", [128, 14], F32)
    esc14 = P.sb("esc14", [128, 14], F32)
    zrow = P.sb("zrow", [128, 1], F32)
    wrow = P.sb("wrow", [128, 1], F32)
    scl_sb = P.sb("scl_sb", [128, 2, 14], F32)

    PSf = P.ps("PSf", [128, 6, 512], F32)
    PSb = P.ps("PSb", [128, 2, 1024], BF16)

    P.inc(gp.dma_start(out=wg[:, :, :], in_=rearr_cp(io["Wg"])), "ld", 16)
    P.inc(gp.dma_start(out=wo[:, :, :], in_=rearr_cp(io["Wo"])), "ld", 16)
    for nm, buf in (("gnorm_rep", gnr), ("utmask", utf), ("identf", identf)):
        P.inc(sy.dma_start(out=buf[:, :], in_=io[nm][:, :]), "ldh", 16)
    for nm, buf in (("utmaskb", utb), ("identb", identb), ("onesrow", onesrow)):
        P.inc(sy.dma_start(out=buf[:, :], in_=io[nm][:, :]), "ldh", 16)
    P.inc(sy.dma_start(out=rw[:, :, :], in_=rearr_cp(io["router_w"])), "ldh", 16)
    for i in range(7):
        P.inc(sy.dma_start(out=SkR[:, :, :].rearrange("p h m -> p (h m)") if i == 0 else skic[:, 0, :],
                           in_=io["prevsk"][i, :, :]), "ldh", 16) if False else None
    ve.wait_ge(P.sem("ld"), P.cnt["ld"])
    ve.wait_ge(P.sem("ldh"), P.cnt["ldh"])
    acc_started = False
    for i in range(7):
        P.inc(sy.dma_start(out=St[:, :].rearrange("p n -> p n") if False else skic[:, 0, :],
                           in_=io["prevsk"][i, :, 0:1024].rearrange("(h p) m -> p (h m)", p=128) if False else io["prevsk"][i, :, :]),
              "pfx", 16) if False else None
    pk = P.sb("pk", [128, 1024], F32)
    pv_ = P.sb("pv_", [128, 2048], F32)
    pss_ = P.sb("pss_", [1, 1024], F32)
    ve.memset(SkR[:, :, :], 0)
    ve.memset(SvR[:, :, :], 0)
    mz = ve.memset(SsR[:, :, :], 0)
    P.edge(mz, ve)
    for i in range(7):
        P.inc(sy.dma_start(out=pk[:, :], in_=io["prevsk"][i, :, :]), f"pfx{i}", 16)
        P.inc(sy.dma_start(out=pv_[:, :], in_=io["prevsv"][i, :, :]), f"pfx{i}", 16)
        P.inc(sy.dma_start(out=pss_[:, :], in_=io["prevss"][i, :, :]), f"pfx{i}", 16)
        ve.wait_ge(P.sem(f"pfx{i}"), P.cnt[f"pfx{i}"])
        ve.tensor_tensor(out=SkR[:, :, :].rearrange("p h m -> p (h m)"),
                         in0=SkR[:, :, :].rearrange("p h m -> p (h m)"), in1=pk[:, :], op=ALU.add)
        ve.tensor_tensor(out=SvR[:, :, :].rearrange("p h m -> p (h m)"),
                         in0=SvR[:, :, :].rearrange("p h m -> p (h m)"), in1=pv_[:, :], op=ALU.add)
        ad = ve.tensor_tensor(out=SsR[:, :, :].rearrange("p b m -> p (b m)"),
                              in0=SsR[:, :, :].rearrange("p b m -> p (b m)"), in1=pss_[:, :], op=ALU.add)
        P.edge(ad, ve)
        if i < 6:
            sy.wait_ge(P.sem("ve_self"), P.cnt["ve_self"])
    ve.tensor_copy(out=SkRb[:, :, :], in_=SkR[:, :, :])
    for bb in range(B):
        for hh in range(H0):
            ve.tensor_copy(out=SvRb[0:64, bb * 4 + hh, :],
                           in_=SvR[(hh % 2) * 64:(hh % 2) * 64 + 64, bb * 2 + hh // 2, :])
    hc = ve.tensor_copy(out=SsRh[:, :, :], in_=SsR[:, :, :])
    P.edge(hc, ve)
    lc = ve.tensor_tensor(out=SsRl[:, :, :], in0=SsR[:, :, :], in1=SsRh[:, :, :], op=ALU.subtract)
    mir = lc
    P.inc(mir, "mir", 1)
    ve.wait_ge(P.sem("ldh"), P.cnt["ldh"])
    rc1 = ve.tensor_copy(out=rwh[:, :, :], in_=rw[:, :, :])
    P.edge(rc1, ve)
    rc2 = ve.tensor_tensor(out=rwl[:, :, :], in0=rw[:, :, :], in1=rwh[:, :, :], op=ALU.subtract)
    P.inc(rc2, "rwrdy", 1)

    for rt in range(NRT):
        b = rt // NCH
        par = rt % 2
        csl = slice(rt * 128, (rt + 1) * 128)
        if rt >= 2:
            gp.wait_ge(P.sem("chunk_done"), rt - 1)
        P.inc(gp.dma_start(out=qfc[:, par, :, :],
                           in_=io["qf"].rearrange("h p n -> p h n")[:, :, csl]), f"cl{par}", 16)
        P.inc(gp.dma_start(out=kfc[:, par, :, :],
                           in_=io["kf"].rearrange("h p n -> p h n")[:, :, csl]), f"cl{par}", 16)
        P.inc(gp.dma_start(out=esc[:, par, :], in_=io["es"][rt, :, :]), f"cl{par}", 16)
        P.inc(gp.dma_start(out=vcc[:, par, :],
                           in_=io["vtm"].rearrange("(r p) c -> p r c", p=128)[:, rt, :]), f"cl{par}", 16)
        P.inc(gp.dma_start(out=xcc[:, par, :, :],
                           in_=rearr_cp(io["xT_slice"])[:, :, csl]), f"cl{par}", 16)
        P.inc(sy.dma_start(out=xrow[:, par, :],
                           in_=io["x_slice"].rearrange("(r p) c -> p r c", p=128)[:, rt, :]), f"clh{par}", 16)
        P.inc(sy.dma_start(out=skic[:, par, :], in_=io["skinc"][rt, :, :]), f"clh{par}", 16)
        P.inc(sy.dma_start(out=svic[:, par, :], in_=io["svinc"][rt, :, :]), f"clh{par}", 16)
        P.inc(sy.dma_start(out=ssic[:, par, :], in_=io["ssinc"][rt, :, :]), f"clh{par}", 16)
        te.wait_ge(P.sem(f"cl{par}"), P.cnt[f"cl{par}"])
        te.wait_ge(P.sem(f"clh{par}"), P.cnt[f"clh{par}"])
        te.wait_ge(P.sem("mir"), P.cnt["mir"])

        if rt >= 1:
            te.wait_ge(P.sem("St_ev"), rt)
        te.matmul(PSf[:, 0, 0:256], utb[:, :], esc[:, par, :], start=True, stop=False)
        te.matmul(PSf[:, 0, 0:256], onesrow[:, :], SsRh[:, b, :], start=False, stop=False)
        mcs = te.matmul(PSf[:, 0, 0:256], onesrow[:, :], SsRl[:, b, :], start=False, stop=True)
        P.inc(mcs, "mm_cs", 1)
        ve.wait_ge(P.sem("mm_cs"), P.cnt["mm_cs"])
        cpS = ve.tensor_copy(out=St[:, :], in_=PSf[:, 0, 0:256])
        P.inc(cpS, "St_ev", 1)
        ve.wait_ge(P.sem("St_ev"), P.cnt["St_ev"])
        rcp = ve.reciprocal(out=rS[:, :], in_=St[:, :])
        P.edge(rcp, ve)

        if rt >= 1:
            te.wait_ge(P.sem("at_ev"), rt * 8)
        for h in range(H0):
            mpt = te.matmul(PSf[:, 1, h * 128:(h + 1) * 128], kfc[:, par, h, :],
                            qfc[:, par, h, :], start=True, stop=True)
        P.inc(mpt, "mm_pt", 1)
        ve.wait_ge(P.sem("mm_pt"), P.cnt["mm_pt"])
        for h in range(H0):
            mk = ve.tensor_tensor(out=ptm[:, h, :], in0=PSf[:, 1, h * 128:(h + 1) * 128],
                                  in1=utf[:, :], op=ALU.mult)
            P.inc(mk, "at_ev", 1)
        ve.wait_ge(P.sem("at_ev"), P.cnt["at_ev"])

        te.wait_ge(P.sem("St_ev"), P.cnt["St_ev"])
        te.wait_ge(P.sem("at_ev"), P.cnt["at_ev"])
        for h in range(H0):
            te.matmul(PSf[:, 0, h * 64:(h + 1) * 64], ptm[:, h, :],
                      esc[:, par, h * 64:(h + 1) * 64], start=True, stop=False)
            mok = te.matmul(PSf[:, 0, h * 64:(h + 1) * 64], qfc[:, par, h, :],
                            SkRb[:, b * 4 + h, :], start=False, stop=True)
        P.inc(mok, "mm_ok", 1)
        ve.wait_ge(P.sem("mm_ok"), P.cnt["mm_ok"])
        dv = ve.tensor_tensor(out=oks[:, :], in0=PSf[:, 0, 0:256], in1=rS[:, :], op=ALU.mult)
        P.inc(dv, "ok_ev", 1)
        ve.wait_ge(P.sem("ok_ev"), P.cnt["ok_ev"])

        for h in range(H0):
            rdm = ve.tensor_reduce(out=mx4[:, h:h + 1], in_=oks[:, h * 64:(h + 1) * 64],
                                   axis=AX.X, op=ALU.max)
        P.edge(rdm, ve)
        ng = ve.tensor_scalar(out=nmx4[:, :], in0=mx4[:, :], scalar1=-1.0, scalar2=None,
                              op0=ALU.mult, op1=ALU.bypass)
        P.inc(ng, "smx", 1)
        sc.wait_ge(P.sem("smx"), P.cnt["smx"])
        for h in range(H0):
            exq = sc.activation(out=qv[:, h * 64:(h + 1) * 64], in_=oks[:, h * 64:(h + 1) * 64],
                                func=AF.Exp, bias=nmx4[:, h:h + 1])
        P.inc(exq, "qve", 1)
        ve.wait_ge(P.sem("qve"), P.cnt["qve"])
        for h in range(H0):
            rds = ve.tensor_reduce(out=sm4[:, h:h + 1], in_=qv[:, h * 64:(h + 1) * 64],
                                   axis=AX.X, op=ALU.add)
        P.edge(rds, ve)
        sm = ve.reciprocal(out=rz4[:, :], in_=sm4[:, :])
        P.edge(sm, ve)
        for h in range(H0):
            qtq = ve.scalar_tensor_tensor(out=qts[:, h * 64:(h + 1) * 64],
                                          in0=qv[:, h * 64:(h + 1) * 64],
                                          scalar=rz4[:, h:h + 1], in1=rS[:, h * 64:(h + 1) * 64],
                                          op0=ALU.mult, op1=ALU.mult)
            if h == H0 - 1:
                P.edge(qtq, ve)
        qcb = ve.tensor_copy(out=qtb[:, :], in_=qts[:, :])
        P.inc(qcb, "qtb_r", 1)

        te.wait_ge(P.sem("qtb_r"), P.cnt["qtb_r"])
        ntr = 0
        for src_, dst_ in ((esc, esf), (qtb, qtf)):
            for h in range(H0):
                if P.cnt.get("trev2", 0) >= 2:
                    te.wait_ge(P.sem("trev2"), P.cnt["trev2"] - 1)
                if src_ is esc:
                    tr = te.transpose(PSb[0:64, ntr % 2, 0:128], esc[:, par, h * 64:(h + 1) * 64],
                                      identb[:, :])
                else:
                    tr = te.transpose(PSb[0:64, ntr % 2, 0:128], qtb[:, h * 64:(h + 1) * 64],
                                      identb[:, :])
                P.inc(tr, "trq", 1)
                sc.wait_ge(P.sem("trq"), P.cnt["trq"])
                ev = sc.copy(out=dst_[0:64, h, :], in_=PSb[0:64, ntr % 2, 0:128])
                P.inc(ev, "trev2", 1)
                ntr += 1
        te.wait_ge(P.sem("trev2"), P.cnt["trev2"])
        for h in range(H0):
            mat = te.matmul(PSf[:, 1, h * 128:(h + 1) * 128], esf[0:64, h, :], qtf[0:64, h, :],
                            start=True, stop=True)
        P.inc(mat, "mm_at", 1)
        ve.wait_ge(P.sem("mm_at"), P.cnt["mm_at"])
        for h in range(H0):
            mk = ve.tensor_tensor(out=atm[:, h, :], in0=PSf[:, 1, h * 128:(h + 1) * 128],
                                  in1=utf[:, :], op=ALU.mult)
            P.inc(mk, "at_ev", 1)
        ve.wait_ge(P.sem("at_ev"), P.cnt["at_ev"])

        te.wait_ge(P.sem("at_ev"), P.cnt["at_ev"])
        if rt >= 1:
            te.wait_ge(P.sem("o_ev"), rt)
            te.wait_ge(P.sem("lg_ev"), rt)
        for h in range(H0):
            bk = 2 + h // 2
            te.matmul(PSf[:, bk, (h % 2) * 256:(h % 2) * 256 + 256], atm[:, h, :],
                      vcc[:, par, h * 256:(h + 1) * 256], start=True, stop=False)
            mov = te.matmul(PSf[:, bk, (h % 2) * 256:(h % 2) * 256 + 256],
                            qtf[0:64, h, :],
                            SvRb[0:64, b * 4 + h, :],
                            start=False, stop=True)
        P.inc(mov, "mm_ov", 1)

        ve.wait_ge(P.sem("mm_ov"), P.cnt["mm_ov"])
        up1 = ve.tensor_tensor(out=SkR[:, b * 4:(b + 1) * 4, :].rearrange("p h m -> p (h m)"),
                               in0=SkR[:, b * 4:(b + 1) * 4, :].rearrange("p h m -> p (h m)"),
                               in1=skic[:, par, :], op=ALU.add)
        P.edge(up1, ve)
        up1b = ve.tensor_copy(out=SkRb[:, b * 4:(b + 1) * 4, :].rearrange("p h m -> p (h m)"),
                              in_=SkR[:, b * 4:(b + 1) * 4, :].rearrange("p h m -> p (h m)"))
        up2 = ve.tensor_tensor(out=SvR[:, b * 2:(b + 1) * 2, :].rearrange("p h m -> p (h m)"),
                               in0=SvR[:, b * 2:(b + 1) * 2, :].rearrange("p h m -> p (h m)"),
                               in1=svic[:, par, :], op=ALU.add)
        P.edge(up2, ve)
        for hh in range(H0):
            up2b = ve.tensor_copy(out=SvRb[0:64, b * 4 + hh, :],
                                  in_=SvR[(hh % 2) * 64:(hh % 2) * 64 + 64, b * 2 + hh // 2, :])
        up3 = ve.tensor_tensor(out=SsR[:, b, :], in0=SsR[:, b, :], in1=ssic[:, par, :], op=ALU.add)
        P.edge(up3, ve)
        up3h = ve.tensor_copy(out=SsRh[:, b, :], in_=SsR[:, b, :])
        P.edge(up3h, ve)
        up3l = ve.tensor_tensor(out=SsRl[:, b, :], in0=SsR[:, b, :], in1=SsRh[:, b, :],
                                op=ALU.subtract)
        P.inc(up3l, "mir", 1)

        ve.wait_ge(P.sem("mm_ov"), P.cnt["mm_ov"])
        oev = ve.tensor_copy(out=o_sb[:, :].rearrange("p (k n) -> p k n", k=2),
                             in_=PSf[:, 2:4, 0:512])
        P.inc(oev, "o_ev", 1)
        sc.wait_ge(P.sem("o_ev"), P.cnt["o_ev"])
        for h in range(H0):
            sq = sc.activation(out=scr[:, :], in_=o_sb[:, h * 256:(h + 1) * 256],
                               func=AF.Square, accum_out=ssq[:, h:h + 1])
            if h < H0 - 1:
                P.edge(sq, sc, "sc_self")
        P.inc(sq, "ssq_r", 1)
        ve.wait_ge(P.sem("ssq_r"), P.cnt["ssq_r"])
        vts = ve.tensor_scalar(out=rstd4[:, :], in0=ssq[:, :], scalar1=1.0 / HV, scalar2=1e-5,
                               op0=ALU.mult, op1=ALU.add)
        P.inc(vts, "rst1", 1)
        sc.wait_ge(P.sem("rst1"), P.cnt["rst1"])
        sq2 = sc.activation(out=rstd4[:, :], in_=rstd4[:, :], func=AF.Sqrt)
        P.inc(sq2, "rst2", 1)
        ve.wait_ge(P.sem("rst2"), P.cnt["rst2"])
        rc3 = ve.reciprocal(out=rstd4[:, :], in_=rstd4[:, :])
        P.edge(rc3, ve)
        for h in range(H0):
            ots = ve.tensor_scalar(out=o_sb[:, h * 256:(h + 1) * 256],
                                   in0=o_sb[:, h * 256:(h + 1) * 256],
                                   scalar1=rstd4[:, h:h + 1], scalar2=None,
                                   op0=ALU.mult, op1=ALU.bypass)
            P.edge(ots, ve)
        gnm = ve.tensor_tensor(out=o_sb[:, :], in0=o_sb[:, :], in1=gnr[:, :], op=ALU.mult)
        P.edge(gnm, ve)

        if rt >= 1:
            te.wait_ge(P.sem("x1_ev"), rt)
        for fb in range(2):
            for dc in range(8):
                mg = te.matmul(PSf[:, 4 + fb, :], xcc[:, par, dc, :],
                               wg[:, dc, fb * 512:(fb + 1) * 512],
                               start=(dc == 0), stop=(dc == 7))
        P.inc(mg, "mm_g", 1)
        sc.wait_ge(P.sem("mm_g"), P.cnt["mm_g"])
        sg = sc.activation(out=g_sb[:, :].rearrange("p (k n) -> p k n", k=2),
                           in_=PSf[:, 4:6, 0:512], func=AF.Silu)
        P.inc(sg, "g_ev", 1)
        ve.wait_ge(P.sem("g_ev"), P.cnt["g_ev"])
        gt = ve.tensor_tensor(out=o_sb[:, :], in0=o_sb[:, :], in1=g_sb[:, :], op=ALU.mult)
        P.edge(gt, ve)
        ocb = ve.tensor_copy(out=ob2[:, :], in_=o_sb[:, :])
        P.inc(ocb, "ob2_r", 1)

        te.wait_ge(P.sem("ob2_r"), P.cnt["ob2_r"])
        for ic in range(8):
            if P.cnt.get("trev2", 0) >= 2:
                te.wait_ge(P.sem("trev2"), P.cnt["trev2"] - 1)
            tro = te.transpose(PSb[:, ic % 2, 0:128], ob2[:, ic * 128:(ic + 1) * 128], identb[:, :])
            P.inc(tro, "trq", 1)
            sc.wait_ge(P.sem("trq"), P.cnt["trq"])
            evo = sc.copy(out=ofm[:, ic, :], in_=PSb[:, ic % 2, 0:128])
            P.inc(evo, "trev2", 1)

        te.wait_ge(P.sem("trev2"), P.cnt["trev2"])
        te.wait_ge(P.sem("g_ev"), P.cnt["g_ev"])
        for fb in range(2):
            for ic in range(8):
                mao = te.matmul(PSf[:, 4 + fb, :], ofm[:, ic, :],
                                wo[:, ic, fb * 512:(fb + 1) * 512],
                                start=(ic == 0), stop=(ic == 7))
        P.inc(mao, "mm_ao", 1)

        ve.wait_ge(P.sem("mm_ao"), P.cnt["mm_ao"])
        xa = ve.tensor_tensor(out=x1pre[:, :].rearrange("p (k n) -> p k n", k=2),
                              in0=PSf[:, 4:6, 0:512],
                              in1=xrow[:, par, :].rearrange("p (k n) -> p k n", k=2), op=ALU.add)
        P.inc(xa, "x1_ev", 1)
        ve.wait_ge(P.sem("x1_ev"), P.cnt["x1_ev"])
        b1s = ve.bn_stats(out=bn6[:, :], in_=x1pre[:, 0:512])
        P.edge(b1s, ve)
        b2s = ve.bn_stats(out=bn6[:, :], in_=x1pre[:, 512:1024]) if False else None
        bn6b2 = bn6
        b2s = ve.bn_stats(out=scr[:, 0:6], in_=x1pre[:, 512:1024])
        P.edge(b2s, ve)
        cpb = ve.tensor_copy(out=lg_sb[:, 0:6], in_=bn6[:, :])
        P.edge(cpb, ve)
        cpb2 = ve.tensor_copy(out=lg_sb[:, 6:12], in_=scr[:, 0:6])
        P.edge(cpb2, ve)
        ag = ve.bn_aggr(out=mv2[:, :], in_=lg_sb[:, 0:12])
        P.edge(ag, ve)
        vts2 = ve.tensor_scalar(out=rstdln[:, :], in0=mv2[:, 1:2], scalar1=1e-5, scalar2=None,
                                op0=ALU.add, op1=ALU.bypass)
        P.inc(vts2, "ln1", 1)
        sc.wait_ge(P.sem("ln1"), P.cnt["ln1"])
        sq3 = sc.activation(out=rstdln[:, :], in_=rstdln[:, :], func=AF.Sqrt)
        P.inc(sq3, "ln2", 1)
        ve.wait_ge(P.sem("ln2"), P.cnt["ln2"])
        rc4 = ve.reciprocal(out=rstdln[:, :], in_=rstdln[:, :])
        P.edge(rc4, ve)
        if P.cnt.get(f"x1st{par}", 0):
            ve.wait_ge(P.sem(f"x1st{par}"), P.cnt[f"x1st{par}"])
        if P.cnt.get("trx", 0):
            ve.wait_ge(P.sem("trx"), P.cnt["trx"])
        x1f = ve.tensor_scalar(out=x1row[:, par, :], in0=x1pre[:, :], scalar1=mv2[:, 0:1],
                               scalar2=rstdln[:, 0:1], op0=ALU.subtract, op1=ALU.mult)
        P.inc(x1f, "x1r", 1)
        sy.wait_ge(P.sem("x1r"), P.cnt["x1r"])
        P.inc(sy.dma_start(out=io["x1"].rearrange("(r p) c -> p r c", p=128)[:, rt, :],
                           in_=x1row[:, par, :]), f"x1st{par}", 16)

        te.wait_ge(P.sem("x1r"), P.cnt["x1r"])
        te.wait_ge(P.sem("ok_ev"), P.cnt["ok_ev"])
        te.wait_ge(P.sem("at_ev"), P.cnt["at_ev"])
        if P.cnt.get(f"x1tst{par}", 0):
            ve.wait_ge(P.sem(f"x1tst{par}"), P.cnt[f"x1tst{par}"])
        if P.cnt.get("mm_lg", 0) > 1:
            ve.wait_ge(P.sem("mm_lg"), P.cnt["mm_lg"] - 1)
        for dc in range(8):
            bkt = dc % 2
            if dc >= 2:
                te.wait_ge(P.sem("xtev"), P.cnt["xtev"] - 1)
            trx = te.transpose(PSf[:, bkt, 0:128], x1row[:, par, dc * 128:(dc + 1) * 128], identf[:, :])
            P.inc(trx, "trx", 1)
            ve.wait_ge(P.sem("trx"), P.cnt["trx"])
            hv1 = ve.tensor_copy(out=x1th[:, par, dc, :], in_=PSf[:, bkt, 0:128])
            P.edge(hv1, ve)
            hv2 = ve.tensor_tensor(out=x1tl[:, par, dc, :], in0=PSf[:, bkt, 0:128],
                                   in1=x1th[:, par, dc, :], op=ALU.subtract)
            P.inc(hv2, "xtev", 1)
        gp.wait_ge(P.sem("xtev"), P.cnt["xtev"])
        P.inc(gp.dma_start(out=io["x1t"].rearrange("d p n -> p d n")[:, :, csl],
                           in_=x1th[:, par, :, :]), f"x1tst{par}", 16)
        te.wait_ge(P.sem("o_ev"), P.cnt["o_ev"])
        te.wait_ge(P.sem("rwrdy"), 1)
        te.wait_ge(P.sem("xtev"), P.cnt["xtev"])
        first = True
        for dc in range(8):
            te.matmul(PSf[:, 2, 0:14], x1th[:, par, dc, :], rwh[:, dc, :], start=first, stop=False)
            first = False
            te.matmul(PSf[:, 2, 0:14], x1th[:, par, dc, :], rwl[:, dc, :], start=False, stop=False)
            mlg = te.matmul(PSf[:, 2, 0:14], x1tl[:, par, dc, :], rwh[:, dc, :], start=False,
                            stop=(dc == 7))
        P.inc(mlg, "mm_lg", 1)
        ve.wait_ge(P.sem("mm_lg"), P.cnt["mm_lg"])
        lcp = ve.tensor_copy(out=lg_sb[:, :], in_=PSf[:, 2, 0:14])
        P.inc(lcp, "lg_ev", 1)
        ve.wait_ge(P.sem("lg_ev"), P.cnt["lg_ev"])
        rmx = ve.tensor_reduce(out=wrow[:, :], in_=lg_sb[:, :], axis=AX.X, op=ALU.max)
        P.edge(rmx, ve)
        gemask = ve.tensor_scalar(out=emsk[:, :], in0=lg_sb[:, :], scalar1=wrow[:, 0:1],
                                  scalar2=None, op0=ALU.is_ge, op1=ALU.bypass)
        P.edge(gemask, ve)
        ngm = ve.tensor_scalar(out=zrow[:, :], in0=wrow[:, :], scalar1=-1.0, scalar2=None,
                               op0=ALU.mult, op1=ALU.bypass)
        P.inc(ngm, "lgx", 1)
        sc.wait_ge(P.sem("lgx"), P.cnt["lgx"])
        exl = sc.activation(out=esc14[:, :], in_=lg_sb[:, :], func=AF.Exp, bias=zrow[:, 0:1])
        P.inc(exl, "lge", 1)
        ve.wait_ge(P.sem("lge"), P.cnt["lge"])
        rs2 = ve.tensor_reduce(out=zrow[:, :], in_=esc14[:, :], axis=AX.X, op=ALU.add)
        P.edge(rs2, ve)
        rc5 = ve.reciprocal(out=wrow[:, :], in_=zrow[:, :])
        P.edge(rc5, ve)
        if P.cnt.get(f"sclst{par}", 0):
            ve.wait_ge(P.sem(f"sclst{par}"), P.cnt[f"sclst{par}"])
        sclf = ve.tensor_scalar(out=scl_sb[:, par, :], in0=emsk[:, :], scalar1=wrow[:, 0:1],
                                scalar2=None, op0=ALU.mult, op1=ALU.bypass)
        P.inc(sclf, "sclr", 1)
        sy.wait_ge(P.sem("sclr"), P.cnt["sclr"])
        P.inc(sy.dma_start(out=io["scales"].rearrange("(r p) c -> p r c", p=128)[:, rt, :],
                           in_=scl_sb[:, par, :]), f"sclst{par}", 16)
        mrk = ve.tensor_copy(out=zrow[:, :], in_=zrow[:, :])
        P.inc(mrk, "chunk_done", 1)
    for s in ("x1st0", "x1st1", "x1tst0", "x1tst1", "sclst0", "sclst1"):
        if P.cnt.get(s, 0):
            sy.wait_ge(P.sem(s), P.cnt[s])


def build_launch2():
    P = Prog("l2")
    io = {}
    io["qf"] = P.din("qf", [4, 128, NTOK], BF16)
    io["kf"] = P.din("kf", [4, 128, NTOK], BF16)
    io["vtm"] = P.din("vtm", [NTOK, 1024], BF16)
    io["es"] = P.din("es", [NRT, 128, 256], BF16)
    io["skinc"] = P.din("skinc", [NRT, 128, 256])
    io["svinc"] = P.din("svinc", [NRT, 128, 512])
    io["ssinc"] = P.din("ssinc", [NRT, 1, 256])
    io["prevsk"] = P.din("prevsk", [7, 128, 1024])
    io["prevsv"] = P.din("prevsv", [7, 128, 2048])
    io["prevss"] = P.din("prevss", [7, 1, 1024])
    io["x_slice"] = P.din("x_slice", [NTOK, D])
    io["xT_slice"] = P.din("xT_slice", [D, NTOK])
    io["Wg"] = P.din("Wg", [D, D])
    io["Wo"] = P.din("Wo", [D, D])
    io["gnorm_rep"] = P.din("gnorm_rep", [128, 1024])
    io["router_w"] = P.din("router_w", [D, 14])
    io["utmask"] = P.din("utmask", [128, 128])
    io["utmaskb"] = P.din("utmaskb", [128, 128], BF16)
    io["identb"] = P.din("identb", [128, 128], BF16)
    io["identf"] = P.din("identf", [128, 128])
    io["onesrow"] = P.din("onesrow", [1, 128], BF16)
    io["x1"] = P.dout("x1", [NTOK, D])
    io["x1t"] = P.dout("x1t", [8, 128, NTOK], BF16)
    io["scales"] = P.dout("scales", [NTOK, 14])
    with P.es:
        with P.nc.Block() as block:
            @block.sync
            def _(sy):
                emit_phase2(P, io)
    return P, io


def emit_phase3(P, io):
    nc = P.nc
    sy, gp, te, ve, sc = nc.sync, nc.gpsimd, nc.tensor, nc.vector, nc.scalar
    GT, TT_ = 512, 4
    NG = 8

    w1b = P.sb("w1b", [128, 8, HMOE], BF16)
    w2b = P.sb("w2b", [128, 16, D], BF16)
    xgb = [P.sb(f"xgb{p}", [128, 8, GT], BF16) for p in range(2)]
    hid = [P.sb("hid0", [128, 16, GT], BF16)] * 2
    b1b = P.sb("b1b", [128, 16], F32)
    b2b = P.sb("b2b", [1, NE, D], BF16)
    onesr = P.sb("onesrS", [1, 128], BF16)
    sclb = P.sb("sclb", [128, 32, NE], F32)
    accb = P.sb("accb", [128, 32, D], BF16)
    x1r2 = P.sb("x1r2", [128, 1, D], F32)
    x2pre = P.sb("x2pre", [128, D], F32)
    bn6a = P.sb("bn6a", [128, 6], F32)
    bn6b = P.sb("bn6b", [128, 12], F32)
    mv2 = P.sb("mv2", [128, 2], F32)
    rstd = P.sb("rstd", [128, 1], F32)
    x2row = P.sb("x2row", [128, 1, D], F32)

    pup = P.ps("pup", [128, 2, 512], F32)
    pdn = P.ps("pdn", [128, 2, 2, 512], F32)

    P.inc(gp.dma_start(out=sclb[:, :, :],
                       in_=io["scales"].rearrange("(g p) e -> p g e", p=128)), "ld", 16)
    P.inc(gp.dma_start(out=b2b[:, :, :], in_=io["b2"][None if False else 0:1, :, :]
                       if False else io["b2"].rearrange("e d -> (e d)").rearrange("(o n) -> o n", o=1).rearrange("o (e d) -> o e d", e=NE)), "ld", 16)
    P.inc(gp.dma_start(out=onesr[:, :], in_=io["onesrow"][:, :]), "ld", 16)
    mz = ve.memset(accb[:, :, :], 0)
    P.inc(mz, "ini", 1)
    ve.wait_ge(P.sem("ld"), P.cnt["ld"])
    te.wait_ge(P.sem("ini"), 1)
    te.wait_ge(P.sem("ld"), P.cnt["ld"])

    for e in range(NE):
        if e >= 1:
            gp.wait_ge(P.sem("s_up"), e * 128)
            gp.wait_ge(P.sem("s_dn"), e * 32)
            gp.wait_ge(P.sem("s_hid"), e * 128)
        P.inc(gp.dma_start(out=w1b[:, :, :],
                           in_=io["w1"][e].rearrange("(c p) f -> p c f", p=128)), f"s_w{e % 2}", 16)
        P.inc(gp.dma_start(out=w2b[:, :, :],
                           in_=io["w2"][e].rearrange("(c p) f -> p c f", p=128)), f"s_w{e % 2}", 16)
        P.inc(gp.dma_start(out=b1b[:, :], in_=io["b1r"][e]), f"s_w{e % 2}", 16)
        for g in range(NG):
            idx = e * NG + g
            gpar = idx % 2
            if idx >= 2:
                gp.wait_ge(P.sem("s_up"), (idx - 1) * 16)
            P.inc(gp.dma_start(out=xgb[gpar][:, :, :],
                               in_=io["x1t"].rearrange("d p n -> p d n")[:, :, g * GT:(g + 1) * GT]),
                  f"s_x{gpar}", 16)
            te.wait_ge(P.sem(f"s_w{e % 2}"), P.cnt[f"s_w{e % 2}"])
            te.wait_ge(P.sem(f"s_x{gpar}"), P.cnt[f"s_x{gpar}"])
            for hb in range(16):
                bank = hb % 2
                if idx * 16 + hb >= 2:
                    te.wait_ge(P.sem("s_hid"), idx * 16 + hb - 1)
                for dc in range(8):
                    mi = te.matmul(pup[:, bank, :], w1b[:, dc, hb * 128:(hb + 1) * 128],
                                   xgb[gpar][:, dc, :], start=(dc == 0), stop=(dc == 7))
                P.inc(mi, "s_up", 1)
                sc.wait_ge(P.sem("s_up"), P.cnt["s_up"])
                if idx >= 1:
                    sc.wait_ge(P.sem("s_dn"), (idx - 1) * 4 + 4)
                hv = sc.activation(out=hid[gpar][:, hb, :], in_=pup[:, bank, :],
                                   func=AF.Gelu, bias=b1b[:, hb:hb + 1])
                P.inc(hv, "s_hid", 1)
            te.wait_ge(P.sem("s_hid"), idx * 16 + 16)
            for tt in range(TT_):
                bk = tt % 2
                if idx * TT_ + tt >= 2:
                    te.wait_ge(P.sem("s_acc"), idx * TT_ + tt - 1)
                for fb in range(2):
                    for hc in range(16):
                        te.matmul(pdn[:, bk, fb, :],
                                  hid[gpar][:, hc, tt * 128:(tt + 1) * 128],
                                  w2b[:, hc, fb * 512:(fb + 1) * 512],
                                  start=(hc == 0), stop=False)
                    mi = te.matmul(pdn[:, bk, fb, :], onesr[:, :],
                                   b2b[0:1, e, fb * 512:(fb + 1) * 512],
                                   start=False, stop=True)
                P.inc(mi, "s_dn", 1)
                ve.wait_ge(P.sem("s_dn"), P.cnt["s_dn"])
                row = g * TT_ + tt
                for fb in range(2):
                    ins = ve.scalar_tensor_tensor(
                        out=accb[:, row, fb * 512:(fb + 1) * 512],
                        in0=pdn[:, bk, fb, :],
                        scalar=sclb[:, row, e:e + 1],
                        in1=accb[:, row, fb * 512:(fb + 1) * 512],
                        op0=ALU.mult, op1=ALU.add)
                    if fb == 0:
                        P.edge(ins, ve)
                P.inc(ins, "s_acc", 1)
    ve.wait_ge(P.sem("s_acc"), NE * NG * TT_)
    for rt in range(NRT):
        par = 0
        if P.cnt.get("x2r", 0):
            sy.wait_ge(P.sem("x2r"), P.cnt["x2r"])
        P.inc(sy.dma_start(out=x1r2[:, par, :],
                           in_=io["x1"].rearrange("(r p) c -> p r c", p=128)[:, rt, :]), f"x1l{par}", 16)
        ve.wait_ge(P.sem(f"x1l{par}"), P.cnt[f"x1l{par}"])
        xa = ve.tensor_tensor(out=x2pre[:, :], in0=x1r2[:, par, :], in1=accb[:, rt, :], op=ALU.add)
        P.edge(xa, ve)
        b1s = ve.bn_stats(out=bn6a[:, :], in_=x2pre[:, 0:512])
        P.edge(b1s, ve)
        cp1 = ve.tensor_copy(out=bn6b[:, 0:6], in_=bn6a[:, :])
        P.edge(cp1, ve)
        b2s = ve.bn_stats(out=bn6a[:, :], in_=x2pre[:, 512:1024])
        P.edge(b2s, ve)
        cp2 = ve.tensor_copy(out=bn6b[:, 6:12], in_=bn6a[:, :])
        P.edge(cp2, ve)
        ag = ve.bn_aggr(out=mv2[:, :], in_=bn6b[:, :])
        P.edge(ag, ve)
        v1 = ve.tensor_scalar(out=rstd[:, :], in0=mv2[:, 1:2], scalar1=1e-5, scalar2=None,
                              op0=ALU.add, op1=ALU.bypass)
        P.inc(v1, "ln1", 1)
        sc.wait_ge(P.sem("ln1"), P.cnt["ln1"])
        s1 = sc.activation(out=rstd[:, :], in_=rstd[:, :], func=AF.Sqrt)
        P.inc(s1, "ln2", 1)
        ve.wait_ge(P.sem("ln2"), P.cnt["ln2"])
        r1 = ve.reciprocal(out=rstd[:, :], in_=rstd[:, :])
        P.edge(r1, ve)
        if P.cnt.get(f"x2st{par}", 0):
            ve.wait_ge(P.sem(f"x2st{par}"), P.cnt[f"x2st{par}"])
        x2f = ve.tensor_scalar(out=x2row[:, par, :], in0=x2pre[:, :], scalar1=mv2[:, 0:1],
                               scalar2=rstd[:, 0:1], op0=ALU.subtract, op1=ALU.mult)
        P.inc(x2f, "x2r", 1)
        sy.wait_ge(P.sem("x2r"), P.cnt["x2r"])
        P.inc(sy.dma_start(out=io["x2"].rearrange("(r p) c -> p r c", p=128)[:, rt, :],
                           in_=x2row[:, par, :]), f"x2st{par}", 16)
    for s in ("x2st0", "x2st1"):
        if P.cnt.get(s, 0):
            sy.wait_ge(P.sem(s), P.cnt[s])


def build_launch3():
    P = Prog("l3")
    io = {}
    io["x1"] = P.din("x1", [NTOK, D])
    io["x1t"] = P.din("x1t", [8, 128, NTOK], BF16)
    io["scales"] = P.din("scales", [NTOK, NE])
    io["w1"] = P.din("w1", [NE, D, HMOE])
    io["w2"] = P.din("w2", [NE, HMOE, D])
    io["b1r"] = P.din("b1r", [NE, 128, 16])
    io["b2"] = P.din("b2", [NE, D])
    io["onesrow"] = P.din("onesrow", [1, 128], BF16)
    io["x2"] = P.dout("x2", [NTOK, D])
    with P.es:
        with P.nc.Block() as block:
            @block.sync
            def _(sy):
                emit_phase3(P, io)
    return P, io


def emit_phase4(P, io):
    nc = P.nc
    sy, gp, te, ve, sc = nc.sync, nc.gpsimd, nc.tensor, nc.vector, nc.scalar

    xb = P.sb("xb", [128, 2, B, D], F32)
    ttr_scr = P.sb("ttr_scr", [128, D], F32)
    sc16 = P.sb("sc16", [128, 16], F32)
    mx4 = P.sb("mxp4", [128, 4], F32)
    nmx4 = P.sb("nmxp4", [128, 4], F32)
    ex16 = P.sb("ex16", [128, 16], F32)
    sm4 = P.sb("smp4", [128, 4], F32)
    rz4 = P.sb("rzp4", [128, 4], F32)
    aw = P.sb("aw", [128, 16], F32)
    acc4 = P.sb("acc4", [128, B, D], F32)
    x3pre = P.sb("x3pre", [128, D], F32)
    bn6a = P.sb("p4bn6a", [128, 6], F32)
    bn6b = P.sb("p4bn6b", [128, 12], F32)
    mv2 = P.sb("p4mv2", [128, 2], F32)
    rstd = P.sb("p4rstd", [128, 1], F32)
    x3row = P.sb("x3row", [128, D], F32)
    x3stg = P.sb("x3stg", [128, 8, 128], F32)
    identf = P.sb("identfS4", [128, 128], F32)

    PSf = P.ps("PSf4", [128, 2, 512], F32)

    P.inc(sy.dma_start(out=identf[:, :], in_=io["identf"][:, :]), "ldh", 16)
    ve.wait_ge(P.sem("ldh"), P.cnt["ldh"])
    te.wait_ge(P.sem("ldh"), P.cnt["ldh"])

    for tl in range(NCH):
        par = tl % 2
        if tl >= 2:
            sy.wait_ge(P.sem("p4done"), tl - 1)
        for b in range(B):
            P.inc(sy.dma_start(out=xb[:, par, b, :],
                               in_=io["x2"].rearrange("(r p) c -> p r c", p=128)[:, b * NCH + tl, :]),
                  f"xl{par}", 16)
        ve.wait_ge(P.sem(f"xl{par}"), P.cnt[f"xl{par}"])
        for i in range(B):
            for j in range(i, B):
                tt = ve.tensor_tensor(out=ttr_scr[:, :], in0=xb[:, par, i, :],
                                      in1=xb[:, par, j, :], op=ALU.mult)
                P.edge(tt, ve)
                rr2 = ve.tensor_scalar(out=ttr_scr[:, 0:1] if False else ttr_scr[:, 0:D],
                                       in0=ttr_scr[:, :], scalar1=1.0 / 32.0, scalar2=None,
                                       op0=ALU.mult, op1=ALU.add,
                                       accum_out=sc16[:, i * 4 + j:i * 4 + j + 1]) if False else \
                      ve.tensor_scalar(out=ttr_scr[:, :], in0=ttr_scr[:, :], scalar1=1.0 / 32.0,
                                       scalar2=None, op0=ALU.mult, op1=ALU.add,
                                       accum_out=sc16[:, i * 4 + j:i * 4 + j + 1])
                P.edge(rr2, ve)
                if i != j:
                    cpx = ve.tensor_copy(out=sc16[:, j * 4 + i:j * 4 + i + 1],
                                         in_=sc16[:, i * 4 + j:i * 4 + j + 1])
                    P.edge(cpx, ve)
        for i in range(B):
            rd = ve.tensor_reduce(out=mx4[:, i:i + 1], in_=sc16[:, i * 4:(i + 1) * 4],
                                  axis=AX.X, op=ALU.max)
        P.edge(rd, ve)
        ng = ve.tensor_scalar(out=nmx4[:, :], in0=mx4[:, :], scalar1=-1.0, scalar2=None,
                              op0=ALU.mult, op1=ALU.bypass)
        P.inc(ng, "p4m", 1)
        sc.wait_ge(P.sem("p4m"), P.cnt["p4m"])
        for i in range(B):
            exi = sc.activation(out=ex16[:, i * 4:(i + 1) * 4], in_=sc16[:, i * 4:(i + 1) * 4],
                                func=AF.Exp, bias=nmx4[:, i:i + 1])
            if i < B - 1:
                P.edge(exi, sc, "sc_self")
        P.inc(exi, "p4e", 1)
        ve.wait_ge(P.sem("p4e"), P.cnt["p4e"])
        for i in range(B):
            rs_ = ve.tensor_reduce(out=sm4[:, i:i + 1], in_=ex16[:, i * 4:(i + 1) * 4],
                                   axis=AX.X, op=ALU.add)
        P.edge(rs_, ve)
        rc = ve.reciprocal(out=rz4[:, :], in_=sm4[:, :])
        P.edge(rc, ve)
        for i in range(B):
            sa = ve.tensor_scalar(out=aw[:, i * 4:(i + 1) * 4], in0=ex16[:, i * 4:(i + 1) * 4],
                                  scalar1=rz4[:, i:i + 1], scalar2=None, op0=ALU.mult, op1=ALU.bypass)
        P.edge(sa, ve)
        for i in range(B):
            ci = ve.tensor_scalar(out=acc4[:, i, :], in0=xb[:, par, 0, :],
                                  scalar1=aw[:, i * 4:i * 4 + 1], scalar2=None,
                                  op0=ALU.mult, op1=ALU.bypass)
            P.edge(ci, ve)
            for j in range(1, B):
                ci = ve.scalar_tensor_tensor(out=acc4[:, i, :], in0=xb[:, par, j, :],
                                             scalar=aw[:, i * 4 + j:i * 4 + j + 1],
                                             in1=acc4[:, i, :], op0=ALU.mult, op1=ALU.add)
                P.edge(ci, ve)
        for b in range(B):
            rt = b * NCH + tl
            xa = ve.tensor_tensor(out=x3pre[:, :], in0=xb[:, par, b, :], in1=acc4[:, b, :], op=ALU.add)
            P.edge(xa, ve)
            b1s = ve.bn_stats(out=bn6a[:, :], in_=x3pre[:, 0:512])
            P.edge(b1s, ve)
            cp1 = ve.tensor_copy(out=bn6b[:, 0:6], in_=bn6a[:, :])
            P.edge(cp1, ve)
            b2s = ve.bn_stats(out=bn6a[:, :], in_=x3pre[:, 512:1024])
            P.edge(b2s, ve)
            cp2 = ve.tensor_copy(out=bn6b[:, 6:12], in_=bn6a[:, :])
            P.edge(cp2, ve)
            ag = ve.bn_aggr(out=mv2[:, :], in_=bn6b[:, :])
            P.edge(ag, ve)
            v1 = ve.tensor_scalar(out=rstd[:, :], in0=mv2[:, 1:2], scalar1=1e-5, scalar2=None,
                                  op0=ALU.add, op1=ALU.bypass)
            P.inc(v1, "p4l1", 1)
            sc.wait_ge(P.sem("p4l1"), P.cnt["p4l1"])
            s1 = sc.activation(out=rstd[:, :], in_=rstd[:, :], func=AF.Sqrt)
            P.inc(s1, "p4l2", 1)
            ve.wait_ge(P.sem("p4l2"), P.cnt["p4l2"])
            r1 = ve.reciprocal(out=rstd[:, :], in_=rstd[:, :])
            P.edge(r1, ve)
            if P.cnt.get("x3st", 0):
                ve.wait_ge(P.sem("x3st"), P.cnt["x3st"])
            if P.cnt.get("p4tr", 0):
                ve.wait_ge(P.sem("p4tr"), P.cnt["p4tr"])
            x3f = ve.tensor_scalar(out=x3row[:, :], in0=x3pre[:, :], scalar1=mv2[:, 0:1],
                                   scalar2=rstd[:, 0:1], op0=ALU.subtract, op1=ALU.mult)
            P.inc(x3f, "x3r", 1)
            sy.wait_ge(P.sem("x3r"), P.cnt["x3r"])
            P.inc(sy.dma_start(out=io["x3"].rearrange("(r p) c -> p r c", p=128)[:, rt, :],
                               in_=x3row[:, :]), "x3st", 16)
            te.wait_ge(P.sem("x3r"), P.cnt["x3r"])
            for dc in range(8):
                if dc >= 2:
                    te.wait_ge(P.sem("p4te"), P.cnt["p4te"] - 1)
                trx = te.transpose(PSf[:, dc % 2, 0:128], x3row[:, dc * 128:(dc + 1) * 128],
                                   identf[:, :])
                P.inc(trx, "p4tr", 1)
                ve.wait_ge(P.sem("p4tr"), P.cnt["p4tr"])
                if dc == 0 and P.cnt.get("x3tst", 0):
                    ve.wait_ge(P.sem("x3tst"), P.cnt["x3tst"])
                ev = ve.tensor_copy(out=x3stg[:, dc, :], in_=PSf[:, dc % 2, 0:128])
                P.inc(ev, "p4te", 1)
            gp.wait_ge(P.sem("p4te"), P.cnt["p4te"])
            P.inc(gp.dma_start(out=io["x3t"].rearrange("d p n -> p d n")[:, :, rt * 128:(rt + 1) * 128],
                               in_=x3stg[:, :, :]), "x3tst", 16)
        mk = ve.tensor_copy(out=rstd[:, :], in_=rstd[:, :])
        P.inc(mk, "p4done", 1)
    for s in ("x3st", "x3tst"):
        if P.cnt.get(s, 0):
            (sy if s == "x3st" else gp).wait_ge(P.sem(s), P.cnt[s])


def build_launch4():
    P = Prog("l4")
    io = {}
    io["x2"] = P.din("x2", [NTOK, D])
    io["identf"] = P.din("identf", [128, 128])
    io["x3"] = P.dout("x3", [NTOK, D])
    io["x3t"] = P.dout("x3t", [8, 128, NTOK])
    with P.es:
        with P.nc.Block() as block:
            @block.sync
            def _(sy):
                emit_phase4(P, io)
    return P, io


def emit_p5_attn(P, io):
    nc = P.nc
    sy, gp, te, ve, sc = nc.sync, nc.gpsimd, nc.tensor, nc.vector, nc.scalar
    NX = 1537

    wqkv = P.sb("wqkvS", [128, 8, 1536], BF16)
    wo = P.sb("woS5", [128, 4, 1024], BF16)
    seg2 = P.sb("seg2S", [128, 2], BF16)
    rep2 = P.sb("rep2S", [2, 128], BF16)
    rotT = P.sb("rotTS", [128, 128], BF16)
    identb = P.sb("identbS5", [128, 128], BF16)
    identf = P.sb("identfS5", [128, 128], F32)
    onesr64 = P.sb("onesr64S", [1, 64], BF16)
    qs2 = P.sb("qs2S", [128, 1], F32)
    ks2 = P.sb("ks2S", [128, 1], F32)
    xcq = P.sb("xcqS", [128, 1024], F32)
    xsq = P.sb("xsqS", [128, 1024], F32)
    xck = P.sb("xckS", [128, 1536], F32)
    xsk = P.sb("xskS", [128, 1536], F32)
    bm = P.sb("bmS", [128, 4, 1024], BF16)
    bm0 = P.sb("bm0S", [128, 4, 1024], BF16)

    yown = P.sb("yown", [128, 8, TC], F32)
    ybxb = P.sb("ybxb", [128, 8, NX], BF16)
    qsb = P.sb("qsb", [128, 4, TC], BF16)
    ksb = P.sb("ksb", [128, 4, 1536], BF16)
    vsb = P.sb("vsb", [128, 4, 1536], BF16)
    vtm5 = P.sb("vtm5", [128, 12, 512], BF16)
    q_f = P.sb("q_f", [128, 512], F32)
    q_f2 = P.sb("q_f2", [128, 512], F32)
    sqb = P.sb("sqb", [128, 512], BF16)
    ss2 = P.sb("ss2", [2, 512], F32)
    ssb2 = P.sb("ssb2", [2, 512], BF16)
    qnb = P.sb("qnb", [128, 512], BF16)
    sim_sb = P.sb("sim_sb", [128, 1024], F32)
    rmax = P.sb("rmax", [128, 1], F32)
    nrmax = P.sb("nrmax", [128, 1], F32)
    p_sb = P.sb("p_sb", [128, 1024], BF16)
    pm_sb = P.sb("pm_sb", [128, 1024], BF16)
    rsum = P.sb("rsum", [128, 4], F32)
    rrec = P.sb("rrec", [128, 4], F32)
    ptb = P.sb("ptb", [128, 8, 512], BF16)
    rrow = P.sb("rrow", [1, 512], F32)
    rrowb = P.sb("rrowb", [1, 512], BF16)
    otf = P.sb("otf", [64, 512], F32)
    ofm5 = P.sb("ofm5", [128, 4, TC], BF16)

    PSf = P.ps("PSf5", [128, 6, 512], F32)
    PSb = P.ps("PSb5", [128, 2, 1024], BF16)

    P.inc(gp.dma_start(out=wqkv[:, :, :], in_=rearr_cp(io["wqkv_l"])), "ld", 16)
    P.inc(gp.dma_start(out=wo[:, :, :], in_=io["wo_l"].rearrange("(c p) f -> p c f", p=128)), "ld", 16)
    for nm, buf in (("seg2", seg2), ("rep2", rep2), ("rotT", rotT), ("identb", identb),
                    ("identf", identf), ("onesr64", onesr64), ("qscale2", qs2),
                    ("kscale2", ks2), ("xcq", xcq), ("xsq", xsq), ("xck", xck), ("xsk", xsk)):
        P.inc(sy.dma_start(out=buf[:, :], in_=io[nm][:, :]), "ldh", 16)
    P.inc(sy.dma_start(out=bm[:, :, :], in_=io["bandm"].rearrange("i p n -> p i n")), "ldh", 16)
    P.inc(sy.dma_start(out=bm0[:, :, :], in_=io["bandm0"].rearrange("i p n -> p i n")), "ldh", 16)
    for e in (te, ve, sc):
        e.wait_ge(P.sem("ld"), P.cnt["ld"])
        e.wait_ge(P.sem("ldh"), P.cnt["ldh"])

    for b in range(B):
        if b > 0:
            gp.wait_ge(P.sem("ydone"), b)
            gp.wait_ge(P.sem("yst"), P.cnt["yst"])
        P.inc(gp.dma_start(out=yown[:, :, :],
                           in_=io["yin"].rearrange("d p n -> p d n")[:, :, b * TC:(b + 1) * TC]),
              f"yl{b}", 16)
        P.inc(gp.dma_start(out=ybxb[:, :, 0:513],
                           in_=io["halo"].rearrange("d p n -> p d n")[:, :, b * 513:(b + 1) * 513]),
              f"yl{b}", 16)
        P.inc(gp.dma_start(out=ybxb[:, :, 513:NX],
                           in_=io["yin"].rearrange("d p n -> p d n")[:, :, b * TC:(b + 1) * TC]),
              f"yl{b}", 16)
        te.wait_ge(P.sem(f"yl{b}"), P.cnt[f"yl{b}"])

        for grp in range(3):
            i0 = 1 + grp * 512
            for fci in range(12):
                is_q = fci < 4
                if is_q and grp == 0:
                    continue
                bank = (grp * 12 + fci) % 2
                nq = P.cnt.get("qkv_ev_v", 0) + P.cnt.get("qkv_ev_s", 0)
                if nq >= 2:
                    for s_ in ("qkv_ev_v", "qkv_ev_s"):
                        if P.cnt.get(s_, 0):
                            te.wait_ge(P.sem(s_), max(0, P.cnt[s_] - 1))
                for dc in range(8):
                    off = i0 if dc < 4 else i0 - 1
                    mi = te.matmul(PSf[:, bank, :], wqkv[:, dc, fci * 128:(fci + 1) * 128],
                                   ybxb[:, dc, off:off + 512], start=(dc == 0), stop=(dc == 7))
                P.inc(mi, "mm_qkv", 1)
                if fci >= 8:
                    sc.wait_ge(P.sem("mm_qkv"), P.cnt["mm_qkv"])
                    ev = sc.copy(out=vsb[:, fci - 8, grp * 512:(grp + 1) * 512], in_=PSf[:, bank, :])
                    P.inc(ev, "qkv_ev_s", 1)
                else:
                    kq_sb = qsb if is_q else ksb
                    colq = (grp - 1) * 512 if is_q else grp * 512
                    tabc = xcq if is_q else xck
                    tabs = xsq if is_q else xsk
                    tcol = (grp - 1) * 512 if is_q else grp * 512
                    scl2 = qs2 if is_q else ks2
                    sc.wait_ge(P.sem("mm_qkv"), P.cnt["mm_qkv"])
                    sq = sc.activation(out=sqb[:, :], in_=PSf[:, bank, :], func=AF.Square)
                    P.inc(sq, "sq_r", 1)
                    te.wait_ge(P.sem("sq_r"), P.cnt["sq_r"])
                    for s_ in ("ss1", "rot_used", "rrow_ev"):
                        if P.cnt.get(s_, 0):
                            te.wait_ge(P.sem(s_), P.cnt[s_])
                    ms = te.matmul(PSf[0:2, 4, 0:512], seg2[:, :], sqb[:, :], start=True, stop=True)
                    P.inc(ms, "mm_ss", 1)
                    ve.wait_ge(P.sem("mm_ss"), P.cnt["mm_ss"])
                    c1 = ve.tensor_scalar(out=ss2[:, :], in0=PSf[0:2, 4, 0:512], scalar1=1e-24,
                                          scalar2=None, op0=ALU.add, op1=ALU.bypass)
                    P.inc(c1, "ss1", 1)
                    sc.wait_ge(P.sem("ss1"), P.cnt["ss1"])
                    l1 = sc.activation(out=ss2[:, :], in_=ss2[:, :], func=AF.Ln)
                    P.edge(l1, sc, "sc_self")
                    l2 = sc.activation(out=ssb2[:, :], in_=ss2[:, :], func=AF.Exp, scale=-0.5)
                    P.inc(l2, "ss2", 1)
                    te.wait_ge(P.sem("ss2"), P.cnt["ss2"])
                    if P.cnt.get("qn_r", 0):
                        te.wait_ge(P.sem("qn_r"), P.cnt["qn_r"])
                    mr = te.matmul(PSf[:, 5, 0:512], rep2[0:2, :], ssb2[:, :], start=True, stop=True)
                    P.inc(mr, "mm_rep", 1)
                    ve.wait_ge(P.sem("mm_rep"), P.cnt["mm_rep"])
                    q1 = ve.tensor_copy(out=q_f[:, :], in_=PSf[:, bank, :])
                    P.edge(q1, ve)
                    q2 = ve.scalar_tensor_tensor(out=q_f[:, :], in0=PSf[:, 5, 0:512],
                                                 scalar=scl2[:, 0:1], in1=q_f[:, :],
                                                 op0=ALU.mult, op1=ALU.mult)
                    P.inc(q2, "qn_r", 1)
                    ve.wait_ge(P.sem("qn_r"), P.cnt["qn_r"])
                    qc = ve.tensor_copy(out=qnb[:, :], in_=q_f[:, :])
                    P.inc(qc, "qnb_r", 1)
                    te.wait_ge(P.sem("qnb_r"), P.cnt["qnb_r"])
                    if P.cnt.get("rot_used", 0):
                        te.wait_ge(P.sem("rot_used"), P.cnt["rot_used"])
                    if P.cnt.get("ss1", 0):
                        te.wait_ge(P.sem("ss1"), P.cnt["ss1"])
                    mrot = te.matmul(PSf[:, 4, 0:512], rotT[:, :], qnb[:, :], start=True, stop=True)
                    P.inc(mrot, "mm_rot", 1)
                    ve.wait_ge(P.sem("qnb_r"), P.cnt["qnb_r"])
                    t1 = ve.tensor_tensor(out=q_f[:, :], in0=q_f[:, :],
                                          in1=tabc[:, tcol:tcol + 512], op=ALU.mult)
                    P.edge(t1, ve)
                    ve.wait_ge(P.sem("mm_rot"), P.cnt["mm_rot"])
                    t2 = ve.tensor_tensor(out=q_f2[:, :], in0=PSf[:, 4, 0:512],
                                          in1=tabs[:, tcol:tcol + 512], op=ALU.mult)
                    P.inc(t2, "rot_used", 1)
                    ve.wait_ge(P.sem("rot_used"), P.cnt["rot_used"])
                    if is_q:
                        t3 = ve.tensor_tensor(out=q_f[:, :], in0=q_f[:, :], in1=q_f2[:, :], op=ALU.add)
                        P.edge(t3, ve)
                        t4 = ve.tensor_scalar(out=kq_sb[:, fci % 4, colq:colq + 512], in0=q_f[:, :],
                                              scalar1=float(QK_SCALE), scalar2=None,
                                              op0=ALU.mult, op1=ALU.bypass)
                    else:
                        t4 = ve.tensor_tensor(out=kq_sb[:, fci % 4, colq:colq + 512],
                                              in0=q_f[:, :], in1=q_f2[:, :], op=ALU.add)
                    P.inc(t4, "qkv_ev_v", 1)
        for s_ in ("qkv_ev_v", "qkv_ev_s"):
            if P.cnt.get(s_, 0):
                te.wait_ge(P.sem(s_), P.cnt[s_])
        for jc in range(12):
            for vc in range(4):
                if P.cnt.get("vt_ev", 0) >= 2:
                    te.wait_ge(P.sem("vt_ev"), P.cnt["vt_ev"] - 1)
                tr = te.transpose(PSb[:, (jc * 4 + vc) % 2, 0:128],
                                  vsb[:, vc, jc * 128:(jc + 1) * 128], identb[:, :])
                P.inc(tr, "vt_q", 1)
                sc.wait_ge(P.sem("vt_q"), P.cnt["vt_q"])
                ev = sc.copy(out=vtm5[:, jc, vc * 128:(vc + 1) * 128],
                             in_=PSb[:, (jc * 4 + vc) % 2, 0:128])
                P.inc(ev, "vt_ev", 1)
        te.wait_ge(P.sem("vt_ev"), P.cnt["vt_ev"])

        for hd in range(HEADS):
            ch = hd // 2
            base = (hd % 2) * 64
            for w in range(2):
                for ic in range(4):
                    bsim = ic % 2
                    if P.cnt.get("sim_ev", 0) >= 2:
                        te.wait_ge(P.sem("sim_ev"), P.cnt["sim_ev"] - 1)
                    for jh in range(2):
                        mi = te.matmul(PSf[:, bsim * 2 + jh, 0:512],
                                       qsb[base:base + 64, ch,
                                           w * 512 + ic * 128:w * 512 + (ic + 1) * 128],
                                       ksb[base:base + 64, ch,
                                           w * 512 + jh * 512:w * 512 + (jh + 1) * 512],
                                       start=True, stop=True)
                    P.inc(mi, "mm_sim", 1)
                    ve.wait_ge(P.sem("mm_sim"), P.cnt["mm_sim"])
                    e1 = ve.tensor_scalar(out=sim_sb[:, :].rearrange("p (k n) -> p k n", k=2),
                                          in0=PSf[:, bsim * 2:bsim * 2 + 2, 0:512], scalar1=1.0,
                                          scalar2=None, op0=ALU.mult, op1=ALU.max,
                                          accum_out=rmax[:, 0:1])
                    P.inc(e1, "sim_ev", 1)
                    ve.wait_ge(P.sem("sim_ev"), P.cnt["sim_ev"])
                    ng = ve.tensor_scalar(out=nrmax[:, :], in0=rmax[:, :], scalar1=-1.0,
                                          scalar2=None, op0=ALU.mult, op1=ALU.bypass)
                    P.inc(ng, "nmx_r", 1)
                    sc.wait_ge(P.sem("nmx_r"), P.cnt["nmx_r"])
                    ex = sc.activation(out=p_sb[:, :], in_=sim_sb[:, :], func=AF.Exp,
                                       bias=nrmax[:, 0:1])
                    P.inc(ex, "p_r", 1)
                    ve.wait_ge(P.sem("p_r"), P.cnt["p_r"])
                    mtile = bm0 if w == 0 else bm
                    mm2 = ve.scalar_tensor_tensor(out=pm_sb[:, :], in0=p_sb[:, :], scalar=1.0,
                                                  in1=mtile[:, ic, :], op0=ALU.mult, op1=ALU.mult,
                                                  accum_out=rsum[:, ic:ic + 1])
                    P.inc(mm2, "pm_r", 1)
                    te.wait_ge(P.sem("pm_r"), P.cnt["pm_r"])
                    for jc in range(8):
                        if P.cnt.get("pt_ev", 0) >= 2:
                            te.wait_ge(P.sem("pt_ev"), P.cnt["pt_ev"] - 1)
                        tr = te.transpose(PSb[:, jc % 2, 0:128], pm_sb[:, jc * 128:(jc + 1) * 128],
                                          identb[:, :])
                        P.inc(tr, "pt_q", 1)
                        sc.wait_ge(P.sem("pt_q"), P.cnt["pt_q"])
                        ev = sc.copy(out=ptb[:, jc, ic * 128:(ic + 1) * 128],
                                     in_=PSb[:, jc % 2, 0:128])
                        P.inc(ev, "pt_ev", 1)
                ve.wait_ge(P.sem("pt_ev"), P.cnt["pt_ev"])
                rc = ve.reciprocal(out=rrec[:, :], in_=rsum[:, :])
                P.inc(rc, "rr_r", 1)
                te.wait_ge(P.sem("rr_r"), P.cnt["rr_r"])
                for s_ in ("rot_used", "ss1", "rrow_ev"):
                    if P.cnt.get(s_, 0):
                        te.wait_ge(P.sem(s_), P.cnt[s_])
                for ic in range(4):
                    mtr = te.transpose(PSf[0:1, 4, ic * 128:(ic + 1) * 128], rrec[:, ic:ic + 1],
                                       identf[:, :])
                    P.inc(mtr, "rt_q", 1)
                ve.wait_ge(P.sem("rt_q"), P.cnt["rt_q"])
                cr = ve.tensor_copy(out=rrow[:, :], in_=PSf[0:1, 4, 0:512])
                P.edge(cr, ve)
                crb = ve.tensor_copy(out=rrowb[:, :], in_=rrow[:, :])
                P.inc(crb, "rrb_r", 1)
                te.wait_ge(P.sem("rrb_r"), P.cnt["rrb_r"])
                if P.cnt.get("o_r", 0):
                    te.wait_ge(P.sem("o_r"), P.cnt["o_r"])
                for jc in range(8):
                    jcx = w * 4 + jc
                    mpv = te.matmul(PSf[0:64, 5, 0:512], vtm5[:, jcx, hd * 64:(hd + 1) * 64],
                                    ptb[:, jc, :], start=(jc == 0), stop=(jc == 7))
                P.inc(mpv, "mm_pv", 1)
                mrp = te.matmul(PSf[0:64, 4, 0:512], onesr64[:, :], rrowb[:, :], start=True, stop=True)
                P.inc(mrp, "rrow_ev", 1)
                ve.wait_ge(P.sem("mm_pv"), P.cnt["mm_pv"])
                oc = ve.tensor_copy(out=otf[:, :], in_=PSf[0:64, 5, 0:512])
                P.edge(oc, ve)
                ve.wait_ge(P.sem("rrow_ev"), P.cnt["rrow_ev"])
                onr = ve.tensor_tensor(out=otf[:, :], in0=otf[:, :], in1=PSf[0:64, 4, 0:512],
                                       op=ALU.mult)
                P.edge(onr, ve)
                ocb = ve.tensor_copy(out=ofm5[base:base + 64, ch, w * 512:(w + 1) * 512],
                                     in_=otf[:, :])
                P.inc(ocb, "o_r", 1)
        te.wait_ge(P.sem("o_r"), P.cnt["o_r"])
        for dmc in range(8):
            for tg in range(2):
                bko = (dmc * 2 + tg) % 2
                if P.cnt.get("yup_r", 0) >= 2:
                    te.wait_ge(P.sem("yup_r"), P.cnt["yup_r"] - 1)
                for ic in range(4):
                    mwo = te.matmul(PSf[:, bko, :], wo[:, ic, dmc * 128:(dmc + 1) * 128],
                                    ofm5[:, ic, tg * 512:(tg + 1) * 512], start=(ic == 0),
                                    stop=(ic == 3))
                P.inc(mwo, "mm_wo", 1)
                ve.wait_ge(P.sem("mm_wo"), P.cnt["mm_wo"])
                ya = ve.tensor_tensor(out=yown[:, dmc, tg * 512:(tg + 1) * 512],
                                      in0=PSf[:, bko, :],
                                      in1=yown[:, dmc, tg * 512:(tg + 1) * 512], op=ALU.add)
                P.inc(ya, "yup_r", 1)
        ve.wait_ge(P.sem("yup_r"), P.cnt["yup_r"])
        sy.wait_ge(P.sem("yup_r"), P.cnt["yup_r"])
        P.inc(sy.dma_start(out=io["yout"].rearrange("d p n -> p d n")[:, :, b * TC:(b + 1) * TC],
                           in_=yown[:, :, :]), "yst", 16)
        mk = ve.tensor_copy(out=rmax[:, :], in_=rmax[:, :])
        P.inc(mk, "ydone", 1)
        if "dbg_q" in io and b == 0:
            sy.wait_ge(P.sem("qkv_ev_v"), P.cnt["qkv_ev_v"])
            sy.wait_ge(P.sem("qkv_ev_s"), P.cnt["qkv_ev_s"])
            sy.wait_ge(P.sem("o_r"), P.cnt["o_r"])
            P.inc(sy.dma_start(out=io["dbg_q"][:, :, :], in_=qsb[:, :, :]), "dbst", 16)
            P.inc(sy.dma_start(out=io["dbg_k"][:, :, :], in_=ksb[:, :, :]), "dbst", 16)
            P.inc(sy.dma_start(out=io["dbg_v"][:, :, :], in_=vsb[:, :, :]), "dbst", 16)
            P.inc(sy.dma_start(out=io["dbg_o"][:, :, :], in_=ofm5[:, :, :]), "dbst", 16)
            sy.wait_ge(P.sem("dbst"), P.cnt["dbst"])
        if "dbg_q" in io and P.cnt.get("dbst", 0):
            ve.wait_ge(P.sem("dbst"), P.cnt["dbst"])
    sy.wait_ge(P.sem("yst"), P.cnt["yst"])


def build_launch5a():
    P = Prog("l5a")
    io = {}
    io["yin"] = P.din("yin", [8, 128, NTOK])
    io["halo"] = P.din("halo", [8, 128, B * 513])
    io["wqkv_l"] = P.din("wqkv_l", [D, 1536])
    io["wo_l"] = P.din("wo_l", [512, D])
    io["seg2"] = P.din("seg2", [128, 2], BF16)
    io["rep2"] = P.din("rep2", [2, 128], BF16)
    io["rotT"] = P.din("rotT", [128, 128], BF16)
    io["identb"] = P.din("identb", [128, 128], BF16)
    io["identf"] = P.din("identf", [128, 128])
    io["onesr64"] = P.din("onesr64", [1, 64], BF16)
    io["qscale2"] = P.din("qscale2", [128, 1])
    io["kscale2"] = P.din("kscale2", [128, 1])
    io["xcq"] = P.din("xcq", [128, 1024])
    io["xsq"] = P.din("xsq", [128, 1024])
    io["xck"] = P.din("xck", [128, 1536])
    io["xsk"] = P.din("xsk", [128, 1536])
    io["bandm"] = P.din("bandm", [4, 128, 1024], BF16)
    io["bandm0"] = P.din("bandm0", [4, 128, 1024], BF16)
    io["yout"] = P.dout("yout", [8, 128, NTOK])
    io["dbg_q"] = P.dout("dbg_q", [128, 4, TC], BF16)
    io["dbg_k"] = P.dout("dbg_k", [128, 4, 1536], BF16)
    io["dbg_v"] = P.dout("dbg_v", [128, 4, 1536], BF16)
    io["dbg_o"] = P.dout("dbg_o", [128, 4, TC], BF16)
    with P.es:
        with P.nc.Block() as block:
            @block.sync
            def _(sy):
                emit_p5_attn(P, io)
    return P, io


def emit_p5_ffn(P, io):
    nc = P.nc
    sy, gp, te, ve, sc = nc.sync, nc.gpsimd, nc.tensor, nc.vector, nc.scalar

    w2sb = P.sb("w2sb", [128, 22, D], BF16)
    lgb2 = P.sb("lgb2", [2, 8, 128], BF16)
    b1a = P.sb("b1aS", [128, 22], F32)
    b1g = P.sb("b1gS", [128, 22], F32)
    b2r = P.sb("b2rS", [128, 8], F32)
    onescol = P.sb("onescolF", [128, 1], BF16)
    w1c = [P.sb(f"w1c{i}", [128, 8, 128], BF16) for i in range(2)]

    ybx2 = P.sb("ybx2", [128, 8, TC + 1], F32)
    ybx2b = P.sb("ybx2b", [128, 8, TC + 1], BF16)
    sqg = P.sb("sqg", [128, 512], BF16)
    mu_r = P.sb("mu_r", [1, 512], F32)
    var_r = P.sb("var_r", [1, 512], F32)
    rstd_r = P.sb("rstd_r", [1, 512], F32)
    rows2 = P.sb("rows2", [2, 512], BF16)
    rstdb = P.sb("rstdb", [1, 512], BF16)
    tmp5 = P.sb("tmp5", [128, 512], F32)
    h2b = P.sb("h2b", [128, 2, 8, 512], BF16)
    glu = P.sb("glu", [128, 2, 512], F32)
    hf = P.sb("hf", [128, 22, 2, 512], BF16)

    PSf = P.ps("PSf5f", [128, 6, 512], F32)

    P.inc(gp.dma_start(out=w2sb[:, 0:21, :],
                       in_=io["w2f_l"][0:2688, :].rearrange("(c p) f -> p c f", p=128)), "ld", 16)
    P.inc(gp.dma_start(out=w2sb[0:42, 21, :], in_=io["w2f_l"][2688:2730, :]), "ld", 16)
    P.inc(gp.dma_start(out=lgb2[:, :, :],
                       in_=io["lgb_l"].rearrange("two (c p) -> two c p", p=128)), "ld", 16)
    for nm, buf in (("b1a_l", b1a), ("b1g_l", b1g), ("b2r_l", b2r)):
        P.inc(sy.dma_start(out=buf[:, :], in_=io[nm][:, :]), "ldh", 16)
    P.inc(sy.dma_start(out=onescol[:, :], in_=io["onescol"][:, :]), "ldh", 16)
    P.inc(sy.dma_start(out=rows2[1:2, :], in_=io["ones512"][:, :]), "ldh", 16)
    ve.wait_ge(P.sem("ldh"), P.cnt["ldh"])
    for e in (te, sc):
        e.wait_ge(P.sem("ld"), P.cnt["ld"])
        e.wait_ge(P.sem("ldh"), P.cnt["ldh"])

    for b in range(B):
        if b > 0:
            gp.wait_ge(P.sem("ydn5"), b)
            gp.wait_ge(P.sem("yst5"), P.cnt["yst5"])
        P.inc(gp.dma_start(out=ybx2[:, :, 0:1],
                           in_=io["prev1"].rearrange("d p n -> p d n")[:, :, b:b + 1],
                           allow_slow_non_contiguous=True), f"yl{b}", 16)
        P.inc(gp.dma_start(out=ybx2[:, :, 1:TC + 1],
                           in_=io["yin"].rearrange("d p n -> p d n")[:, :, b * TC:(b + 1) * TC]),
              f"yl{b}", 16)
        ve.wait_ge(P.sem(f"yl{b}"), P.cnt[f"yl{b}"])
        cb = ve.tensor_copy(out=ybx2b[:, :, :], in_=ybx2[:, :, :])
        P.inc(cb, "ybb", 1)
        te.wait_ge(P.sem("ybb"), P.cnt["ybb"])
        for grp in range(2):
            g0 = 1 + grp * 512
            if P.cnt.get("st_ev", 0):
                te.wait_ge(P.sem("st_ev"), P.cnt["st_ev"])
            if P.cnt.get("yac5", 0):
                te.wait_ge(P.sem("yac5"), P.cnt["yac5"])
            for dc in range(8):
                off = g0 if dc < 4 else g0 - 1
                m1 = te.matmul(PSf[0:1, 0, 0:512], onescol[:, :], ybx2b[:, dc, off:off + 512],
                               start=(dc == 0), stop=(dc == 7))
            P.inc(m1, "mm_s1", 1)
            for dc in range(8):
                off = g0 if dc < 4 else g0 - 1
                sc.wait_ge(P.sem("ybb"), P.cnt["ybb"])
                if P.cnt.get("mm_s2", 0):
                    sc.wait_ge(P.sem("mm_s2"), P.cnt["mm_s2"])
                sq = sc.activation(out=sqg[:, :], in_=ybx2b[:, dc, off:off + 512], func=AF.Square)
                P.inc(sq, "sq5", 1)
                te.wait_ge(P.sem("sq5"), P.cnt["sq5"])
                m2 = te.matmul(PSf[0:1, 1, 0:512], onescol[:, :], sqg[:, :],
                               start=(dc == 0), stop=(dc == 7))
                P.inc(m2, "mm_s2", 1)
            ve.wait_ge(P.sem("mm_s1"), P.cnt["mm_s1"])
            ve.wait_ge(P.sem("mm_s2"), P.cnt["mm_s2"])
            v1 = ve.tensor_scalar(out=mu_r[:, :], in0=PSf[0:1, 0, 0:512], scalar1=1.0 / D,
                                  scalar2=None, op0=ALU.mult, op1=ALU.bypass)
            P.edge(v1, ve)
            v2 = ve.tensor_scalar(out=var_r[:, :], in0=PSf[0:1, 1, 0:512], scalar1=1.0 / D,
                                  scalar2=None, op0=ALU.mult, op1=ALU.bypass)
            P.edge(v2, ve)
            v3 = ve.tensor_tensor(out=tmp5[0:1, 0:512], in0=mu_r[:, :], in1=mu_r[:, :], op=ALU.mult)
            P.edge(v3, ve)
            v4 = ve.tensor_tensor(out=var_r[:, :], in0=var_r[:, :], in1=tmp5[0:1, 0:512],
                                  op=ALU.subtract)
            P.edge(v4, ve)
            v5 = ve.tensor_scalar(out=var_r[:, :], in0=var_r[:, :], scalar1=1e-5, scalar2=None,
                                  op0=ALU.add, op1=ALU.bypass)
            P.inc(v5, "st1", 1)
            sc.wait_ge(P.sem("st1"), P.cnt["st1"])
            s1 = sc.activation(out=rstd_r[:, :], in_=var_r[:, :], func=AF.Ln)
            P.edge(s1, sc, "sc_self")
            s2 = sc.activation(out=rstd_r[:, :], in_=rstd_r[:, :], func=AF.Exp, scale=-0.5)
            P.inc(s2, "st2", 1)
            ve.wait_ge(P.sem("st2"), P.cnt["st2"])
            v6 = ve.tensor_tensor(out=tmp5[0:1, 0:512], in0=mu_r[:, :], in1=rstd_r[:, :], op=ALU.mult)
            P.edge(v6, ve)
            v7 = ve.tensor_scalar(out=rows2[0:1, :], in0=tmp5[0:1, 0:512], scalar1=-1.0,
                                  scalar2=None, op0=ALU.mult, op1=ALU.bypass)
            P.edge(v7, ve)
            v8 = ve.tensor_copy(out=rstdb[:, :], in_=rstd_r[:, :])
            P.inc(v8, "st3", 1)
            st_cnt = P.cnt["st3"]
            te.wait_ge(P.sem("st3"), st_cnt)
            if P.cnt.get("h2_ev", 0):
                te.wait_ge(P.sem("h2_ev"), P.cnt["h2_ev"])
            for dc in range(8):
                off = g0 if dc < 4 else g0 - 1
                if dc > 0:
                    te.wait_ge(P.sem("h2dc"), P.cnt["h2dc"])
                msc = te.matmul(PSf[:, 2, 0:512], lgb2[0:1, dc, :], rstdb[:, :], start=True, stop=True)
                mbi = te.matmul(PSf[:, 3, 0:512], lgb2[0:2, dc, :], rows2[:, :], start=True, stop=True)
                P.inc(mbi, "mm_h2", 1)
                ve.wait_ge(P.sem("mm_h2"), P.cnt["mm_h2"])
                t1 = ve.tensor_tensor(out=tmp5[:, :], in0=ybx2[:, dc, off:off + 512],
                                      in1=PSf[:, 2, 0:512], op=ALU.mult)
                P.edge(t1, ve)
                t2 = ve.tensor_tensor(out=h2b[:, grp, dc, :], in0=tmp5[:, :], in1=PSf[:, 3, 0:512],
                                      op=ALU.add)
                P.inc(t2, "h2dc", 1)
            mk = ve.tensor_copy(out=mu_r[:, :], in_=mu_r[:, :])
            P.inc(mk, "st_ev", 1)
            P.inc(ve.tensor_copy(out=var_r[:, :], in_=var_r[:, :]), "h2_ev", 1)
        te.wait_ge(P.sem("h2dc"), P.cnt["h2dc"])
        for fc in range(22):
            rows = 42 if fc == 21 else 128
            lw = 2730 + fc * 128
            if P.cnt.get("mm_g5", 0):
                gp.wait_ge(P.sem("mm_g5"), P.cnt["mm_g5"])
            P.inc(gp.dma_start(out=w1c[0][:, :, 0:rows],
                               in_=io["w1f_l"].rearrange("(c p) f -> p c f", p=128)[:, :, lw:lw + rows]),
                  "w1g", 16)
            te.wait_ge(P.sem("w1g"), P.cnt["w1g"])
            for grp in range(2):
                if P.cnt.get("glu_ev", 0) >= 2:
                    te.wait_ge(P.sem("glu_ev"), P.cnt["glu_ev"] - 1)
                for dc in range(8):
                    mg = te.matmul(PSf[0:rows, 4 + grp, 0:512], w1c[0][:, dc, 0:rows],
                                   h2b[:, grp, dc, :], start=(dc == 0), stop=(dc == 7))
                P.inc(mg, "mm_g5", 1)
                sc.wait_ge(P.sem("mm_g5"), P.cnt["mm_g5"])
                if P.cnt.get("hf_ev", 0):
                    sc.wait_ge(P.sem("hf_ev"), P.cnt["hf_ev"])
                gl = sc.activation(out=glu[0:rows, grp, :], in_=PSf[0:rows, 4 + grp, 0:512],
                                   func=AF.Gelu, bias=b1g[0:rows, fc:fc + 1])
                P.inc(gl, "glu_ev", 1)
            aw_ = fc * 128
            if P.cnt.get("mm_a5", 0):
                gp.wait_ge(P.sem("mm_a5"), P.cnt["mm_a5"])
            P.inc(gp.dma_start(out=w1c[1][:, :, 0:rows],
                               in_=io["w1f_l"].rearrange("(c p) f -> p c f", p=128)[:, :, aw_:aw_ + rows]),
                  "w1a", 16)
            te.wait_ge(P.sem("w1a"), P.cnt["w1a"])
            te.wait_ge(P.sem("glu_ev"), P.cnt["glu_ev"])
            te.wait_ge(P.sem("h2dc"), P.cnt["h2dc"])
            for grp in range(2):
                if P.cnt.get("hf_ev", 0) >= 2:
                    te.wait_ge(P.sem("hf_ev"), P.cnt["hf_ev"] - 1)
                for dc in range(8):
                    ma = te.matmul(PSf[0:rows, 2 + grp, 0:512], w1c[1][:, dc, 0:rows],
                                   h2b[:, grp, dc, :], start=(dc == 0), stop=(dc == 7))
                P.inc(ma, "mm_a5", 1)
                ve.wait_ge(P.sem("mm_a5"), P.cnt["mm_a5"])
                hv = ve.scalar_tensor_tensor(out=hf[0:rows, fc, grp, :],
                                             in0=PSf[0:rows, 2 + grp, 0:512],
                                             scalar=b1a[0:rows, fc:fc + 1],
                                             in1=glu[0:rows, grp, :], op0=ALU.add, op1=ALU.mult)
                P.inc(hv, "hf_ev", 1)
        te.wait_ge(P.sem("hf_ev"), P.cnt["hf_ev"])
        for dmc in range(8):
            for grp in range(2):
                bk = (dmc * 2 + grp) % 4
                if P.cnt.get("yac5", 0) >= 4:
                    te.wait_ge(P.sem("yac5"), P.cnt["yac5"] - 3)
                for hc in range(22):
                    rows = 42 if hc == 21 else 128
                    md = te.matmul(PSf[:, (0 if bk < 2 else 2) + bk % 2, 0:512] if False else
                                   PSf[:, bk, 0:512],
                                   w2sb[0:rows, hc, dmc * 128:(dmc + 1) * 128],
                                   hf[0:rows, hc, grp, :], start=(hc == 0), stop=(hc == 21))
                P.inc(md, "mm_d5", 1)
                ve.wait_ge(P.sem("mm_d5"), P.cnt["mm_d5"])
                ya = ve.scalar_tensor_tensor(out=ybx2[:, dmc, 1 + grp * 512:1 + (grp + 1) * 512],
                                             in0=PSf[:, bk, 0:512], scalar=b2r[:, dmc:dmc + 1],
                                             in1=ybx2[:, dmc, 1 + grp * 512:1 + (grp + 1) * 512],
                                             op0=ALU.add, op1=ALU.add)
                P.inc(ya, "yac5", 1)
        ve.wait_ge(P.sem("yac5"), P.cnt["yac5"])
        sy.wait_ge(P.sem("yac5"), P.cnt["yac5"])
        P.inc(sy.dma_start(out=io["yout"].rearrange("d p n -> p d n")[:, :, b * TC:(b + 1) * TC],
                           in_=ybx2[:, :, 1:TC + 1]), "yst5", 16)
        mk2 = ve.tensor_copy(out=mu_r[:, :], in_=mu_r[:, :])
        P.inc(mk2, "ydn5", 1)
    sy.wait_ge(P.sem("yst5"), P.cnt["yst5"])


def build_launch5b():
    P = Prog("l5b")
    io = {}
    io["yin"] = P.din("yin", [8, 128, NTOK])
    io["prev1"] = P.din("prev1", [8, 128, B])
    io["w1f_l"] = P.din("w1f_l", [D, 5460])
    io["w2f_l"] = P.din("w2f_l", [HID1, D])
    io["lgb_l"] = P.din("lgb_l", [2, D])
    io["b1a_l"] = P.din("b1a_l", [128, 22])
    io["b1g_l"] = P.din("b1g_l", [128, 22])
    io["b2r_l"] = P.din("b2r_l", [128, 8])
    io["onescol"] = P.din("onescol", [128, 1], BF16)
    io["ones512"] = P.din("ones512", [1, 512], BF16)
    io["yout"] = P.dout("yout", [8, 128, NTOK])
    with P.es:
        with P.nc.Block() as block:
            @block.sync
            def _(sy):
                emit_p5_ffn(P, io)
    return P, io


def emit_phase6(P, io):
    nc = P.nc
    sy, gp, te, ve, sc = nc.sync, nc.gpsimd, nc.tensor, nc.vector, nc.scalar

    identf = P.sb("identfS6", [128, 128], F32)
    ycol = P.sb("ycol", [128, 2, 8, 128], F32)
    x3r = P.sb("x3r6", [128, 2, D], F32)
    ypre = P.sb("ypre", [128, D], F32)
    bn6a = P.sb("p6bn6a", [128, 6], F32)
    bn6b = P.sb("p6bn6b", [128, 12], F32)
    mv2 = P.sb("p6mv2", [128, 2], F32)
    rstd = P.sb("p6rstd", [128, 1], F32)
    orow = P.sb("orow", [128, 2, D], F32)

    PSf = P.ps("PSf6", [128, 2, 512], F32)

    P.inc(sy.dma_start(out=identf[:, :], in_=io["identf"][:, :]), "ldh", 16)
    te.wait_ge(P.sem("ldh"), P.cnt["ldh"])
    ve.wait_ge(P.sem("ldh"), P.cnt["ldh"])

    for rt in range(NRT):
        par = rt % 2
        if rt >= 2:
            gp.wait_ge(P.sem("p6tr"), (rt - 1) * 8)
        P.inc(gp.dma_start(out=ycol[:, par, :, :],
                           in_=io["y4"].rearrange("d p n -> p d n")[:, :, rt * 128:(rt + 1) * 128]),
              f"p6l{par}", 16)
        P.inc(sy.dma_start(out=x3r[:, par, :],
                           in_=io["x3"].rearrange("(r p) c -> p r c", p=128)[:, rt, :]),
              f"p6h{par}", 16)
        te.wait_ge(P.sem(f"p6l{par}"), P.cnt[f"p6l{par}"])
        for dc in range(8):
            if P.cnt.get("p6te", 0) >= 2:
                te.wait_ge(P.sem("p6te"), P.cnt["p6te"] - 1)
            tr = te.transpose(PSf[:, dc % 2, 0:128], ycol[:, par, dc, :], identf[:, :])
            P.inc(tr, "p6tr", 1)
            ve.wait_ge(P.sem("p6tr"), P.cnt["p6tr"])
            if dc == 0:
                ve.wait_ge(P.sem("p6y"), P.cnt.get("p6y", 0))
                ve.wait_ge(P.sem(f"p6h{par}"), P.cnt[f"p6h{par}"])
            ev = ve.tensor_tensor(out=ypre[:, dc * 128:(dc + 1) * 128], in0=PSf[:, dc % 2, 0:128],
                                  in1=x3r[:, par, dc * 128:(dc + 1) * 128], op=ALU.add)
            P.inc(ev, "p6te", 1)
        ve.wait_ge(P.sem("p6te"), P.cnt["p6te"])
        b1s = ve.bn_stats(out=bn6a[:, :], in_=ypre[:, 0:512])
        P.edge(b1s, ve)
        cp1 = ve.tensor_copy(out=bn6b[:, 0:6], in_=bn6a[:, :])
        P.edge(cp1, ve)
        b2s = ve.bn_stats(out=bn6a[:, :], in_=ypre[:, 512:1024])
        P.edge(b2s, ve)
        cp2 = ve.tensor_copy(out=bn6b[:, 6:12], in_=bn6a[:, :])
        P.edge(cp2, ve)
        ag = ve.bn_aggr(out=mv2[:, :], in_=bn6b[:, :])
        P.edge(ag, ve)
        v1 = ve.tensor_scalar(out=rstd[:, :], in0=mv2[:, 1:2], scalar1=1e-5, scalar2=None,
                              op0=ALU.add, op1=ALU.bypass)
        P.inc(v1, "p6ln1", 1)
        sc.wait_ge(P.sem("p6ln1"), P.cnt["p6ln1"])
        s1 = sc.activation(out=rstd[:, :], in_=rstd[:, :], func=AF.Sqrt)
        P.inc(s1, "p6ln2", 1)
        ve.wait_ge(P.sem("p6ln2"), P.cnt["p6ln2"])
        r1 = ve.reciprocal(out=rstd[:, :], in_=rstd[:, :])
        P.edge(r1, ve)
        if P.cnt.get(f"p6st{par}", 0):
            ve.wait_ge(P.sem(f"p6st{par}"), P.cnt[f"p6st{par}"])
        of = ve.tensor_scalar(out=orow[:, par, :], in0=ypre[:, :], scalar1=mv2[:, 0:1],
                              scalar2=rstd[:, 0:1], op0=ALU.subtract, op1=ALU.mult)
        P.inc(of, "p6y", 1)
        sy.wait_ge(P.sem("p6y"), P.cnt["p6y"])
        P.inc(sy.dma_start(out=io["out"].rearrange("(r p) c -> p r c", p=128)[:, rt, :],
                           in_=orow[:, par, :]), f"p6st{par}", 16)
    for s in ("p6st0", "p6st1"):
        if P.cnt.get(s, 0):
            sy.wait_ge(P.sem(s), P.cnt[s])


def build_launch6():
    P = Prog("l6")
    io = {}
    io["x3"] = P.din("x3", [NTOK, D])
    io["y4"] = P.din("y4", [8, 128, NTOK])
    io["identf"] = P.din("identf", [128, 128])
    io["out"] = P.dout("out", [NTOK, D])
    with P.es:
        with P.nc.Block() as block:
            @block.sync
            def _(sy):
                emit_phase6(P, io)
    return P, io


_PCACHE = {}


def _get_prog(name):
    if name not in _PCACHE:
        _PCACHE[name] = {
            "l1": build_launch1, "l2": build_launch2, "l3": build_launch3,
            "l4": build_launch4, "l5a": build_launch5a, "l5b": build_launch5b,
            "l6": build_launch6,
        }[name]()
    return _PCACHE[name]


def _run(name, maps):
    P, _ = _get_prog(name)
    res = run_bass_kernel_spmd(P.nc, maps, core_ids=list(range(NCORES)))
    return [res.results[c] for c in range(NCORES)]


def kernel(**inputs):
    hosts = [host_core_inputs(c, inputs) for c in range(NCORES)]

    def maps_for(name, extra_fn=None):
        P, _ = _get_prog(name)
        out = []
        for c in range(NCORES):
            m = {}
            for k in P.inputs:
                if extra_fn is not None:
                    v = extra_fn(c, k)
                    if v is not None:
                        m[k] = v
                        continue
                m[k] = hosts[c][k]
            out.append(m)
        return out

    r1 = _run("l1", maps_for("l1"))

    def ex2(c, k):
        if k in ("qf", "kf", "vtm", "es", "skinc", "svinc", "ssinc"):
            return r1[c][k]
        if k == "prevsk":
            a = np.zeros((7, 128, 1024), np.float32)
            for i in range(c):
                a[i] = np.asarray(r1[i]["sktot"])
            return a
        if k == "prevsv":
            a = np.zeros((7, 128, 2048), np.float32)
            for i in range(c):
                a[i] = np.asarray(r1[i]["svtot"])
            return a
        if k == "prevss":
            a = np.zeros((7, 1, 1024), np.float32)
            for i in range(c):
                a[i] = np.asarray(r1[i]["sstot"])
            return a
        return None
    r2 = _run("l2", maps_for("l2", ex2))

    def ex3(c, k):
        if k in ("x1", "x1t", "scales"):
            return r2[c][k]
        return None
    r3 = _run("l3", maps_for("l3", ex3))

    def ex4(c, k):
        if k == "x2":
            return r3[c]["x2"]
        return None
    r4 = _run("l4", maps_for("l4", ex4))

    def attn_maps(li, ycur):
        def ex(c, k):
            if k == "yin":
                return ycur[c]
            if k == "halo":
                if c == 0:
                    return np.zeros((8, 128, B * 513), np.float32)
                prev = np.asarray(ycur[c - 1]).reshape(D, NTOK)
                halo = np.concatenate(
                    [prev[:, (bb + 1) * TC - 513:(bb + 1) * TC] for bb in range(B)], 1)
                return np.ascontiguousarray(halo).reshape(8, 128, B * 513)
            if k == "wqkv_l":
                return hosts[c][f"wqkv_{li}"]
            if k == "wo_l":
                return hosts[c][f"wo_{li}"]
            if k == "qscale2":
                return hosts[c][f"qscale2_{li}"]
            if k == "kscale2":
                return hosts[c][f"kscale2_{li}"]
            return None
        return maps_for("l5a", ex)

    def ffn_maps(li, ycur):
        def ex(c, k):
            if k == "yin":
                return ycur[c]
            if k == "prev1":
                if c == 0:
                    return np.zeros((8, 128, B), np.float32)
                prev = np.asarray(ycur[c - 1]).reshape(D, NTOK)
                cols = np.stack([prev[:, (bb + 1) * TC - 1] for bb in range(B)], 1)
                return np.ascontiguousarray(cols).reshape(8, 128, B)
            if k == "w1f_l":
                return hosts[c][f"w1f_{li}"]
            if k == "w2f_l":
                return hosts[c][f"w2f_{li}"]
            if k == "lgb_l":
                return hosts[c][f"lgb_{li}"]
            if k == "b1a_l":
                return hosts[c][f"b1a_{li}"]
            if k == "b1g_l":
                return hosts[c][f"b1g_{li}"]
            if k == "b2r_l":
                return hosts[c][f"b2r_{li}"]
            return None
        return maps_for("l5b", ex)

    ycur = [np.asarray(r4[c]["x3t"]) for c in range(NCORES)]
    for li in range(2):
        r5a = _run("l5a", attn_maps(li, ycur))
        ycur = [np.asarray(r5a[c]["yout"]) for c in range(NCORES)]
        r5b = _run("l5b", ffn_maps(li, ycur))
        ycur = [np.asarray(r5b[c]["yout"]) for c in range(NCORES)]

    def ex6(c, k):
        if k == "x3":
            return r4[c]["x3"]
        if k == "y4":
            return ycur[c]
        return None
    r6 = _run("l6", maps_for("l6", ex6))

    out = np.zeros((B, T, D), np.float32)
    for c in range(NCORES):
        out[:, c * TC:(c + 1) * TC, :] = np.asarray(r6[c]["out"]).reshape(B, TC, D)
    return out
